# version 12
# baseline (speedup 1.0000x reference)
import numpy as np
import concourse.bass as bass
import concourse.mybir as mybir
from concourse.ap import AP
from concourse.bass_utils import run_bass_kernel_spmd

F32 = mybir.dt.float32
BF16 = mybir.dt.bfloat16
ACT = mybir.ActivationFunctionType
ALU = mybir.AluOpType

TOK = 4096
NBLK = 8
BLK = 512
D = 1024
NJ = 8
NFO = 32
NKT = 32
KC = 128
LC = 32
MIX = 0.05
NSUP = 4
SUP = TOK // NSUP
NS3 = 4
S3 = TOK // NS3

DEBUG_TAPS = ()


class Prog:
    ENGINES = ("sync", "gpsimd", "vector", "scalar", "tensor")

    def __init__(self):
        self.ops = {e: [] for e in self.ENGINES}
        self.cnt = {}
        self.last_wait = {}

    def n(self, sem):
        return self.cnt.get(sem, 0)

    def now(self):
        return [(s, v) for s, v in self.cnt.items() if s != "CC"]

    def op(self, eng, emit, waits=(), inc=None):
        w = []
        for s, v in waits:
            if v <= 0:
                continue
            k = (eng, s)
            if self.last_wait.get(k, -1) >= v:
                continue
            self.last_wait[k] = v
            w.append((s, v))
        if inc is not None:
            s, a = inc
            self.cnt[s] = self.cnt.get(s, 0) + a
        self.ops[eng].append((emit, w, inc))


class Arena:
    def __init__(self, size):
        self.size = size
        self.free = [(0, size)]
        self.tensor = None
        self.named = {}
        self.hw = 0

    def alloc(self, name, nbytes, align=64):
        assert name not in self.named, f"region name reused: {name}"
        nbytes = (nbytes + align - 1) // align * align
        for i, (off, ln) in enumerate(self.free):
            aoff = (off + align - 1) // align * align
            if aoff + nbytes <= off + ln:
                repl = []
                if aoff - off:
                    repl.append((off, aoff - off))
                if off + ln - (aoff + nbytes):
                    repl.append((aoff + nbytes, off + ln - (aoff + nbytes)))
                self.free[i:i + 1] = repl
                self.named[name] = (aoff, nbytes)
                self.hw = max(self.hw, self.size - sum(l for _, l in self.free))
                return aoff
        raise MemoryError(f"arena OOM for {name} ({nbytes}B) free={self.free}")

    def release(self, *names):
        for name in names:
            off, ln = self.named[name]
            self.released = getattr(self, "released", set()) | {name}
            self.free.append((off, ln))
        self.free.sort()
        merged = []
        for off, ln in self.free:
            if merged and merged[-1][0] + merged[-1][1] == off:
                merged[-1] = [merged[-1][0], merged[-1][1] + ln]
            else:
                merged.append([off, ln])
        self.free = [tuple(x) for x in merged]

    def base(self, dtype):
        return self.tensor[:].bitcast(dtype)

    def ap(self, name, dtype, shape, elem_off=0, part=128):
        off, _ = self.named[name]
        es = mybir.dt.size(dtype)
        b = self.base(dtype)
        dims = [[b.ap[0][0], part]]
        stride = 1
        rev = []
        for s in reversed(shape):
            rev.append([stride, s])
            stride *= s
        dims += rev[::-1]
        return AP(b.tensor, b.offset + off // es + elem_off, dims)

    def sap(self, name, dtype, dims, elem_off=0, part=128):
        off, _ = self.named[name]
        es = mybir.dt.size(dtype)
        b = self.base(dtype)
        return AP(b.tensor, b.offset + off // es + elem_off,
                  [[b.ap[0][0], part]] + [list(d) for d in dims])


def build_nc(taps=()):
    nc = bass.Bass("TRN2", target_bir_lowering=False, debug=False)
    taps = set(taps)

    def dram_in(name, shape, dtype=F32):
        return nc.dram_tensor(name, list(shape), dtype, kind="ExternalInput").ap()

    x_hbm = dram_in("x", [D, TOK])
    w_q = dram_in("wq", [NJ, NJ, 128, 128])
    w_k = dram_in("wk", [NJ, NJ, 128, 128])
    w_v = dram_in("wv", [NJ, NJ, 128, 128])
    w_g = dram_in("wg", [NJ, NJ, 128, 128])
    w_o = dram_in("wo", [NJ, NJ, 128, 128])
    w_b = dram_in("wb", [1, NJ, 128, 128])
    w_fd = dram_in("wfd", [1, NJ, 128, 128])
    w_sd = dram_in("wsd", [1, NJ, 128, 128])
    bfd_r = dram_in("bfd", [128, 1])
    bsd_r = dram_in("bsd", [128, 1])
    mask_in = dram_in("mask", [128, 1])
    w_1 = dram_in("w1", [NFO, NJ, 128, 128])
    w_2 = dram_in("w2", [NFO, NJ, 128, 128])
    h_1 = dram_in("h1", [NFO, NKT, 128, 128])
    h_2 = dram_in("h2", [NFO, NKT, 128, 128])
    w_3 = dram_in("w3", [NJ, NKT, 128, 128])
    s_1 = dram_in("s1", [128, NFO])
    s_2 = dram_in("s2", [128, NFO])
    s_h1 = dram_in("sh1", [128, NFO])
    s_h2 = dram_in("sh2", [128, NFO])
    s_3 = dram_in("s3", [128, NJ])

    out_hbm = nc.dram_tensor("out", [D, TOK], F32, kind="ExternalOutput").ap()

    qsp = nc.dram_tensor("qsp", [NJ, 128, TOK], BF16).ap()
    xn1sp = nc.dram_tensor("xn1sp", [NJ, 128, TOK], BF16).ap()
    xn2sp = nc.dram_tensor("xn2sp", [NJ, 128, TOK], BF16).ap()
    x1sp = nc.dram_tensor("x1sp", [NJ, 128, TOK], F32).ap()
    g1sp = nc.dram_tensor("g1sp", [NFO, 128, TOK], BF16).ap()
    h1sp = nc.dram_tensor("h1sp", [NFO, 128, TOK], BF16).ap()
    h2sp = nc.dram_tensor("h2sp", [NFO, 128, TOK], BF16).ap()
    tend_b = nc.dram_tensor("tend_b", [128, 16], F32)
    tcg_b = nc.dram_tensor("tcg_b", [256, 16], F32)

    P = Prog()
    AR = Arena(206 * 1024)
    tap_d = {}

    from contextlib import ExitStack
    es = ExitStack()
    arena_t = es.enter_context(
        nc.sbuf_tensor("arena", [128, AR.size], mybir.dt.uint8))
    AR.tensor = arena_t
    ps = [es.enter_context(nc.psum_tensor(f"ps{i}", [128, 512], F32))
          for i in range(8)]
    sems = {}
    for s in ("MM", "ACT", "VEC", "GP", "CC",
              "XB0", "XB1", "WQA", "WQB", "WG", "WO",
              "FW0", "FW1", "FW2", "FW3", "SCL", "TB",
              "XN1L", "QRL", "XLD", "B2L", "RHA", "RHB", "X1R",
              "XN1S", "X1S", "XN2S", "OUT", "TAP",
              "FS0", "FS1", "FS2", "FS3",
              "QS0", "QS1", "QS2", "QS3", "QS4", "QS5", "QS6", "QS7"):
        sems[s] = es.enter_context(nc.semaphore(s))

    slot_free = [[] for _ in range(8)]

    def tap(name, ap_fn, shape, dtype=F32, waits=()):
        if name not in taps:
            return
        hd = nc.dram_tensor("tap_" + name, list(shape), dtype,
                            kind="ExternalOutput")
        tap_d[name] = hd
        P.op("scalar", lambda s, hd=hd, ap_fn=ap_fn:
             s.dma_start(out=hd.ap(), in_=ap_fn()),
             waits=waits, inc=("TAP", 16))

    AR.alloc("consts", 1024)
    AR.alloc("scales", 4096)
    AR.alloc("statrows", 16 * 1024)
    ones_col = AR.ap("consts", BF16, (1,), elem_off=0)
    ones_row = AR.ap("consts", BF16, (128,), elem_off=64, part=1)
    ones_row_f = AR.ap("consts", F32, (128,), elem_off=96, part=1)

    def scal(off_b, n=1):
        return AR.ap("scales", F32, (n,), elem_off=off_b // 4)

    eps_row = AR.ap("consts", F32, (1,), elem_off=232, part=1)
    neghalf = AR.ap("consts", F32, (1,), elem_off=233, part=1)
    mixc = AR.ap("consts", F32, (1,), elem_off=234)
    mix2c = AR.ap("consts", F32, (1,), elem_off=235)
    P.op("vector", lambda v: (v.memset(ones_col, 1.0),
                              v.memset(ones_row, 1.0),
                              v.memset(ones_row_f, 1.0),
                              v.memset(eps_row, 1e-5),
                              v.memset(neghalf, -0.5),
                              v.memset(mixc, MIX),
                              v.memset(mix2c, MIX * MIX))[-1], inc=("VEC", 1))
    consts_vec = P.n("VEC")
    for off_b, src, n in ((0, s_1, 32), (128, s_2, 32), (256, s_h1, 32),
                          (384, s_h2, 32), (512, s_3, 8), (640, bfd_r, 1),
                          (672, bsd_r, 1), (704, mask_in, 1)):
        P.op("sync", lambda s, o=off_b, sr=src, n=n:
             s.dma_start(out=scal(o, n), in_=sr), inc=("SCL", 16))
    consts_ld = P.n("SCL")

    ln_state = {"bc_mm": [0, 0], "stat_mm": [0, 0]}

    def emit_ln3(src3, dst3, st_slot, bank, src_ready,
                 rn=("xsq", "lnt", "statrows")):
        xsq_nm, lnt_nm, sr_nm = rn
        st2 = (st_slot + 3) % 6
        ro = 1024 * bank

        def row(off):
            return AR.ap(sr_nm, F32, (BLK,), elem_off=ro + off, part=1)

        def rowb(off):
            return AR.ap(sr_nm, BF16, (BLK,), elem_off=6144 + ro + off,
                         part=1)

        tmp = AR.ap(sr_nm, F32, (BLK,), elem_off=2048 + 512 * bank, part=1)

        sq_guard = [("MM", ln_state["stat_mm"][bank])]
        for j in range(NJ):
            P.op("scalar", lambda s, j=j, src3=src3, xsq_nm=xsq_nm:
                 s.activation(AR.ap(xsq_nm, BF16, (NJ, BLK))[:, j],
                              src3()[:, j], ACT.Square),
                waits=list(src_ready) + sq_guard, inc=("ACT", 1))
        sq_act = P.n("ACT")

        def e_mm(t, src3=src3, xsq_nm=xsq_nm):
            last = None
            src = src3()
            for j in range(NJ):
                last = t.matmul(ps[st_slot][0:1, :], ones_col, src[:, j],
                                start=(j == 0), stop=(j == NJ - 1))
            for j in range(NJ):
                last = t.matmul(ps[st2][0:1, :], ones_col,
                                AR.ap(xsq_nm, BF16, (NJ, BLK))[:, j],
                                start=(j == 0), stop=(j == NJ - 1))
            return last
        P.op("tensor", e_mm,
             waits=list(src_ready) + [("ACT", sq_act), ("VEC", consts_vec)]
             + slot_free[st_slot] + slot_free[st2], inc=("MM", 1))
        mm_stat = P.n("MM")
        ln_state["stat_mm"][bank] = mm_stat

        def e_rows(v):
            v.tensor_scalar_mul(row(0), ps[st_slot][0:1, :], 1.0 / D)
            v.tensor_scalar_mul(row(512), ps[st2][0:1, :], 1.0 / D)
            v.tensor_tensor(tmp, row(0), row(0), ALU.mult)
            v.tensor_tensor(row(512), row(512), tmp, ALU.subtract)
            return v.tensor_copy(rowb(0), row(0))
        P.op("vector", e_rows,
             waits=[("MM", mm_stat), ("MM", ln_state["bc_mm"][bank])],
             inc=("VEC", 1))
        rows_vec = P.n("VEC")
        slot_free[st_slot] = [("VEC", rows_vec)]
        slot_free[st2] = [("VEC", rows_vec)]

        def e_rstd(s):
            s.activation(tmp, row(512), ACT.Ln, bias=eps_row)
            return s.activation(row(512), tmp, ACT.Exp, scale=neghalf)
        P.op("scalar", e_rstd, waits=[("VEC", rows_vec)], inc=("ACT", 1))
        rstd_act = P.n("ACT")

        def e_bc(t):
            t.matmul(ps[6][:, :], ones_row_f, row(0), start=True, stop=True)
            return t.matmul(ps[7][:, :], ones_row_f, row(512), start=True,
                            stop=True)
        P.op("tensor", e_bc,
             waits=[("ACT", rstd_act)] + slot_free[6] + slot_free[7],
             inc=("MM", 1))
        mm_bc = P.n("MM")
        ln_state["bc_mm"][bank] = mm_bc

        mu_bc = AP(ps[6][:].tensor, ps[6][:].offset,
                   [[ps[6][:].ap[0][0], 128], [0, NJ], [1, BLK]])
        r_bc = AP(ps[7][:].tensor, ps[7][:].offset,
                  [[ps[7][:].ap[0][0], 128], [0, NJ], [1, BLK]])

        def e_apply(v, src3=src3, dst3=dst3, lnt_nm=lnt_nm):
            v.tensor_tensor(AR.ap(lnt_nm, F32, (NJ, BLK)), src3(), mu_bc,
                            ALU.subtract)
            return v.tensor_tensor(dst3(), AR.ap(lnt_nm, F32, (NJ, BLK)),
                                   r_bc, ALU.mult)
        P.op("vector", e_apply, waits=[("MM", mm_bc)], inc=("VEC", 1))
        va = P.n("VEC")
        slot_free[6] = [("VEC", va)]
        slot_free[7] = [("VEC", va)]
        return va

    AR.alloc("wbufA", 16 * 1024)
    AR.alloc("wbufB", 16 * 1024)
    AR.alloc("B", 64 * 1024)
    AR.alloc("xblk", 2 * NJ * BLK * 2)
    AR.alloc("xsq", NJ * BLK * 2)
    AR.alloc("lnt", NJ * BLK * 4)

    def xn1_t(kt, blk):
        return AR.sap("B", BF16, [[1, BLK]], elem_off=kt * TOK + blk * BLK)

    xn1_ready = []
    for blk in range(NBLK):
        buf = blk % 2
        wre = [("VEC", xn1_ready[blk - 2])] if blk >= 2 else []
        xbs = f"XB{buf}"
        P.op("gpsimd", lambda g, blk=blk, buf=buf: g.dma_start(
            out=AR.ap("xblk", BF16, (2, NJ, BLK))[:, buf],
            in_=AP(x_hbm.tensor, blk * BLK,
                   [[TOK, 128], [128 * TOK, NJ], [1, BLK]])),
            waits=wre, inc=(xbs, 16))
        wd = P.n(xbs)
        va = emit_ln3(
            (lambda buf=buf: AR.ap("xblk", BF16, (2, NJ, BLK))[:, buf]),
            (lambda blk=blk: AR.sap("B", BF16, [[TOK, NJ], [1, BLK]],
                                    elem_off=blk * BLK)),
            st_slot=blk % 3, bank=blk % 2,
            src_ready=[(xbs, wd), ("SCL", consts_ld), ("VEC", consts_vec)])
        xn1_ready.append(va)
        P.op("scalar", lambda s, blk=blk: s.dma_start(
            out=AP(xn1sp.tensor, blk * BLK,
                   [[TOK, 128], [128 * TOK, NJ], [1, BLK]]),
            in_=AR.sap("B", BF16, [[TOK, NJ], [1, BLK]], elem_off=blk * BLK)),
            waits=[("VEC", va)], inc=("XN1S", 16))
    xn1_vec = xn1_ready[-1]
    xn1sp_sd = P.n("XN1S")
    p0_done = P.now()
    AR.release("xblk", "xsq", "lnt", "statrows")
    tap("xn1", lambda: AR.ap("B", BF16, (NJ * TOK,)), (128, NJ * TOK), BF16,
        waits=[("VEC", xn1_vec)])

    AR.alloc("C", 64 * 1024)
    AR.alloc("stg", NJ * BLK * 2)
    AR.alloc("vstg", NJ * BLK * 2)
    AR.alloc("fdec", TOK * 2)
    AR.alloc("sdec", TOK * 2)
    AR.alloc("bdec", TOK * 2)

    wnames = ["wbufA", "wbufB"]
    wfree = [[], []]

    wsem = ["WQA", "WQB"]

    def ld_mixw(widx, src, ntile):
        P.op("gpsimd", lambda g, widx=widx, src=src, ntile=ntile: g.dma_start(
            out=AR.ap(wnames[widx], BF16, (ntile, 128)),
            in_=AP(src.tensor, 0, [[128, 128], [128 * 128, ntile], [1, 128]])),
            waits=wfree[widx], inc=(wsem[widx], 16))
        return (wsem[widx], P.n(wsem[widx]))

    def wt(widx, jo, kt):
        return AR.ap(wnames[widx], BF16, (NJ * NJ, 128))[:, jo * NJ + kt]

    def wt1(widx, kt):
        return AR.ap(wnames[widx], BF16, (NJ, 128))[:, kt]

    rq = ld_mixw(0, w_q, NJ * NJ)
    stg_sd = {}
    for jo in range(NJ):
        for blk in range(NBLK):
            slot = (jo * NBLK + blk) % 6

            def e_mm(t, jo=jo, blk=blk, slot=slot):
                last = None
                for kt in range(NJ):
                    last = t.matmul(ps[slot][:, :], wt(0, jo, kt),
                                    xn1_t(kt, blk),
                                    start=(kt == 0), stop=(kt == NJ - 1))
                return last
            P.op("tensor", e_mm,
                 waits=[rq, ("VEC", xn1_ready[blk])] + slot_free[slot],
                 inc=("MM", 1))
            P.op("scalar", lambda s, jo=jo, slot=slot: s.activation(
                AR.ap("stg", BF16, (NJ, BLK))[:, jo], ps[slot][:, :], ACT.Silu),
                waits=[("MM", P.n("MM"))] + stg_sd.get(jo, []),
                inc=("ACT", 1))
            slot_free[slot] = [("ACT", P.n("ACT"))]
            P.op("scalar", lambda s, jo=jo, blk=blk: s.dma_start(
                out=AP(qsp.tensor, jo * 128 * TOK + blk * BLK,
                       [[TOK, 128], [1, BLK]]),
                in_=AR.ap("stg", BF16, (NJ, BLK))[:, jo]),
                inc=(f"QS{jo}", 16))
            stg_sd[jo] = [(f"QS{jo}", P.n(f"QS{jo}"))]
    q_sd = [(f"QS{j}", P.n(f"QS{j}")) for j in range(NJ)]
    wfree[0] = [("MM", P.n("MM"))]

    rk = ld_mixw(1, w_k, NJ * NJ)
    rv = ld_mixw(0, w_v, NJ * NJ)
    kv_guard = {}
    for jo in range(NJ):
        for blk in range(NBLK):
            sk = (jo * NBLK + blk) % 3
            sv = 3 + (jo * NBLK + blk) % 3

            def e_mk(t, jo=jo, blk=blk, slot=sk):
                last = None
                for kt in range(NJ):
                    last = t.matmul(ps[slot][:, :], wt(1, jo, kt),
                                    xn1_t(kt, blk),
                                    start=(kt == 0), stop=(kt == NJ - 1))
                return last
            P.op("tensor", e_mk, waits=[rk, ("VEC", xn1_ready[blk])]
                 + slot_free[sk], inc=("MM", 1))
            mk = P.n("MM")

            def e_mv(t, jo=jo, blk=blk, slot=sv):
                last = None
                for kt in range(NJ):
                    last = t.matmul(ps[slot][:, :], wt(0, jo, kt),
                                    xn1_t(kt, blk),
                                    start=(kt == 0), stop=(kt == NJ - 1))
                return last
            P.op("tensor", e_mv, waits=[rv, ("VEC", xn1_ready[blk])]
                 + slot_free[sv], inc=("MM", 1))
            mv = P.n("MM")
            P.op("scalar", lambda s, jo=jo, slot=sk: s.activation(
                AR.ap("stg", BF16, (NJ, BLK))[:, jo], ps[slot][:, :], ACT.Silu),
                waits=[("MM", mk)] + q_sd + kv_guard.get(jo, []),
                inc=("ACT", 1))
            slot_free[sk] = [("ACT", P.n("ACT"))]
            P.op("scalar", lambda s, jo=jo, slot=sv: s.activation(
                AR.ap("vstg", BF16, (NJ, BLK))[:, jo], ps[slot][:, :], ACT.Silu),
                waits=[("MM", mv)] + kv_guard.get(jo, []),
                inc=("ACT", 1))
            av = P.n("ACT")
            slot_free[sv] = [("ACT", av)]

            def e_kv(v, jo=jo, blk=blk):
                return v.tensor_tensor(
                    AR.sap("C", BF16, [[1, 16], [1024, LC]],
                           elem_off=jo * 128 + blk * 16),
                    AR.sap("stg", BF16, [[LC, 16], [1, LC]],
                           elem_off=jo * BLK),
                    AR.sap("vstg", BF16, [[LC, 16], [1, LC]],
                           elem_off=jo * BLK),
                    ALU.mult)
            P.op("vector", e_kv, waits=[("ACT", av)], inc=("VEC", 1))
            kv_guard[jo] = [("VEC", P.n("VEC"))]
    kv_vec = P.n("VEC")
    wfree[0] = [("MM", P.n("MM"))]
    wfree[1] = [("MM", P.n("MM"))]

    for name, wsrc, bias_off, dstn, widx in (("b", w_b, None, "bdec", 0),
                                             ("f", w_fd, 640, "fdec", 1),
                                             ("s", w_sd, 672, "sdec", 0)):
        rw = ld_mixw(widx, wsrc, NJ)
        for blk in range(NBLK):
            slot = (blk % 3) if widx == 0 else 3 + (blk % 3)

            def e_mm(t, blk=blk, slot=slot, widx=widx):
                last = None
                for kt in range(NJ):
                    last = t.matmul(ps[slot][:, :], wt1(widx, kt),
                                    xn1_t(kt, blk),
                                    start=(kt == 0), stop=(kt == NJ - 1))
                return last
            P.op("tensor", e_mm, waits=[rw, ("VEC", xn1_ready[blk])]
                 + slot_free[slot], inc=("MM", 1))

            def e_ev(s, blk=blk, slot=slot, bias_off=bias_off, dstn=dstn):
                kw = {}
                if bias_off is not None:
                    kw["bias"] = scal(bias_off)
                return s.activation(
                    AR.ap(dstn, BF16, (TOK,))[:, blk * BLK:(blk + 1) * BLK],
                    ps[slot][:, :], ACT.Sigmoid, **kw)
            P.op("scalar", e_ev, waits=[("MM", P.n("MM")), ("SCL", consts_ld)],
                 inc=("ACT", 1))
            slot_free[slot] = [("ACT", P.n("ACT"))]
        wfree[widx] = [("MM", P.n("MM"))]
    dec_act = P.n("ACT")
    mm_p1 = P.n("MM")

    tap("kv", lambda: AR.ap("C", BF16, (NJ * TOK,)), (128, NJ * TOK), BF16,
        waits=[("VEC", kv_vec)])
    tap("fdec", lambda: AR.ap("fdec", BF16, (TOK,)), (128, TOK), BF16,
        waits=[("ACT", dec_act)])
    tap("sdec", lambda: AR.ap("sdec", BF16, (TOK,)), (128, TOK), BF16,
        waits=[("ACT", dec_act)])
    tap("bdec", lambda: AR.ap("bdec", BF16, (TOK,)), (128, TOK), BF16,
        waits=[("ACT", dec_act)])

    p1_done = P.now()
    AR.release("B", "wbufA", "wbufB", "stg", "vstg")
    AR.alloc("wgbuf", 16 * 1024)
    AR.alloc("wobuf", 16 * 1024)
    AR.alloc("scanws", 36 * 1024)
    AR.alloc("prefq", 8 * 1024)
    SW = "scanws"

    def sw(off, n, dtype=F32):
        return AR.ap(SW, dtype, (n,), elem_off=off)

    def swd(off, dims, dtype=F32):
        return AR.sap(SW, dtype, dims, elem_off=off)

    def stb(i, dims=None):
        if dims is None:
            return AR.ap(SW, BF16, (1024,), elem_off=i * 1024)
        return AR.sap(SW, BF16, dims, elem_off=i * 1024)

    JK = [[128, NJ], [1, KC]]

    P.op("gpsimd", lambda g: g.dma_start(
        out=AR.ap("wgbuf", BF16, (NJ * NJ, 128)),
        in_=AP(w_g.tensor, 0, [[128, 128], [128 * 128, NJ * NJ], [1, 128]])),
        waits=p1_done, inc=("WG", 16))
    rg = ("WG", P.n("WG"))
    P.op("gpsimd", lambda g: g.dma_start(
        out=AR.ap("wobuf", BF16, (NJ * NJ, 128)),
        in_=AP(w_o.tensor, 0, [[128, 128], [128 * 128, NJ * NJ], [1, 128]])),
        inc=("WO", 16))
    ro = ("WO", P.n("WO"))

    P.op("vector", lambda v: v.tensor_tensor(
        AR.sap("C", BF16, [[128, NJ], [1, KC], [1024, LC]]),
        AR.sap("C", BF16, [[128, NJ], [1, KC], [1024, LC]]),
        AR.sap("bdec", BF16, [[0, NJ], [LC, KC], [1, LC]]),
        ALU.mult), waits=[("VEC", kv_vec), ("ACT", dec_act)], inc=("VEC", 1))
    u_vec = P.n("VEC")
    tap("u", lambda: AR.ap("C", BF16, (NJ * TOK,)), (128, NJ * TOK), BF16,
        waits=[("VEC", u_vec)])

    def u_l(l):
        return AR.sap("C", BF16, JK, elem_off=l * 1024)

    def f_l(l):
        return AR.sap("fdec", BF16, [[0, NJ], [LC, KC]], elem_off=l)

    def s_l(l):
        return AR.sap("sdec", BF16, [[0, NJ], [LC, KC]], elem_off=l)

    def emit_pass(pass_b, first_waits=()):
        first = True
        for l in range(LC):
            def e_v1(v, l=l, pass_b=pass_b):
                v.tensor_scalar_mul(stb(2), stb(1), MIX)
                v.tensor_scalar_mul(stb(3), stb(0), MIX)
                v.tensor_tensor(stb(2), stb(2), stb(0), ALU.add)
                v.tensor_tensor(stb(3), stb(3), stb(1), ALU.add)
                v.tensor_tensor(stb(4, JK), stb(2, JK), f_l(l), ALU.mult)
                v.tensor_tensor(stb(5, JK), stb(3, JK), s_l(l), ALU.mult)
                v.tensor_tensor(stb(0, JK), stb(4, JK), u_l(l), ALU.add)
                last = v.tensor_tensor(stb(1, JK), stb(5, JK), u_l(l), ALU.add)
                if pass_b:
                    last = v.tensor_tensor(u_l(l), stb(4, JK), stb(5, JK),
                                           ALU.add)
                return last
            w = []
            if first:
                w += list(first_waits) + [("VEC", u_vec)]
                first = False
            P.op("vector", e_v1, waits=w, inc=("VEC", 1))

    P.op("vector", lambda v: (v.memset(stb(0), 0.0),
                              v.memset(stb(1), 0.0))[-1],
         waits=[("VEC", u_vec), ("XN1S", xn1sp_sd)] + q_sd, inc=("VEC", 1))
    emit_pass(False, first_waits=[("ACT", dec_act)])
    passA_gp = P.n("GP")
    passA_vec = P.n("VEC")

    m2 = MIX * MIX

    def tbp(buf, p, dims, extra=0):
        return AR.sap(SW, F32, dims, elem_off=3072 + buf * 2048 + p * 512 + extra)

    for slab in range(4):
        k0 = slab * 32

        def fs_sl(nm, par, k0=k0):
            return AR.sap(nm, BF16, [[LC, 32], [2, 16]], elem_off=k0 * LC + par)

        d1 = [[16, 32], [1, 16]]

        def e_l1(v, k0=k0, fs_sl=fs_sl):
            def cbc(cell):
                return AP(cell.tensor, cell.offset,
                          [[cell.ap[0][0], 128], [0, 32], [0, 16]])
            v.tensor_tensor(tbp(1, 0, d1), fs_sl("fdec", 0), fs_sl("sdec", 0),
                            ALU.add)
            v.tensor_tensor(tbp(1, 1, d1), fs_sl("sdec", 0), cbc(mix2c),
                            ALU.mult)
            v.tensor_tensor(tbp(1, 1, d1), tbp(1, 1, d1), fs_sl("fdec", 0),
                            ALU.add)
            v.tensor_tensor(tbp(1, 2, d1), fs_sl("fdec", 0), cbc(mix2c),
                            ALU.mult)
            v.tensor_tensor(tbp(1, 2, d1), tbp(1, 2, d1), fs_sl("sdec", 0),
                            ALU.add)
            v.tensor_tensor(tbp(0, 0, d1), fs_sl("fdec", 1), tbp(1, 1, d1),
                            ALU.mult)
            v.tensor_tensor(tbp(0, 3, d1), fs_sl("sdec", 1), tbp(1, 2, d1),
                            ALU.mult)
            v.tensor_tensor(tbp(0, 1, d1), fs_sl("fdec", 1), cbc(mixc),
                            ALU.mult)
            v.tensor_tensor(tbp(0, 1, d1), tbp(0, 1, d1), tbp(1, 0, d1),
                            ALU.mult)
            v.tensor_tensor(tbp(0, 2, d1), fs_sl("sdec", 1), cbc(mixc),
                            ALU.mult)
            return v.tensor_tensor(tbp(0, 2, d1), tbp(0, 2, d1),
                                   tbp(1, 0, d1), ALU.mult)
        P.op("gpsimd", e_l1, waits=[("ACT", dec_act), ("VEC", u_vec)],
             inc=("GP", 1))
        n = 16
        buf = 0
        while n > 1:
            nn = n // 2

            def e_lvl(v, buf=buf, n=n, nn=nn):
                def src(p, par):
                    return AR.sap(SW, F32, [[n, 32], [2, nn]],
                                  elem_off=3072 + buf * 2048 + p * 512 + par)

                def dst(p):
                    return AR.sap(SW, F32, [[nn, 32], [1, nn]],
                                  elem_off=3072 + (1 - buf) * 2048 + p * 512)

                def sc(i):
                    return AR.sap(SW, F32, [[nn, 32], [1, nn]],
                                  elem_off=7680 + i * 512)
                last = None
                for ci, b1p, a1p, b2p, a2p in ((0, 0, 0, 1, 2), (1, 0, 1, 1, 3),
                                               (2, 2, 0, 3, 2), (3, 2, 1, 3, 3)):
                    v.tensor_tensor(sc(0), src(b1p, 1), src(a1p, 0), ALU.mult)
                    v.tensor_tensor(sc(1), src(b2p, 1), src(a2p, 0), ALU.mult)
                    last = v.tensor_tensor(dst(ci), sc(0), sc(1), ALU.add)
                return last
            P.op("gpsimd", e_lvl, inc=("GP", 1))
            buf = 1 - buf
            n = nn

        def e_cp(v, buf=buf, k0=k0):
            last = None
            for p in range(4):
                last = v.tensor_copy(
                    sw(7168 + p * KC + k0, 32),
                    AR.sap(SW, F32, [[1, 32]],
                           elem_off=3072 + buf * 2048 + p * 512))
            return last
        P.op("gpsimd", e_cp, inc=("GP", 1))
    atot_gp = P.n("GP")
    atot_vec = P.n("VEC")
    tap("atot", lambda: sw(7168, 512), (128, 512), F32,
        waits=[("VEC", atot_vec)])

    def Pp(buf, p, off=0, n=KC):
        return AR.sap(SW, F32, [[1, n]],
                      elem_off=7680 + buf * 512 + p * KC + off)

    def Qp(buf, comp, off=0, n=KC):
        return AR.sap("prefq", BF16, [[128, NJ], [1, n]],
                      elem_off=buf * 2048 + comp * 1024 + off)

    def e_pi(v):
        for p in range(4):
            v.tensor_copy(Pp(0, p), sw(7168 + p * KC, KC))
        v.tensor_copy(Qp(0, 0), stb(0, JK))
        return v.tensor_copy(Qp(0, 1), stb(1, JK))
    P.op("vector", e_pi, waits=[("GP", atot_gp)], inc=("VEC", 1))

    buf = 0
    d_ = 1
    while d_ < KC:
        def e_hs(v, buf=buf, d_=d_):
            nb = 1 - buf
            n = KC - d_

            def scP(i):
                return AR.sap(SW, F32, [[1, n]], elem_off=5120 + i * 512)

            def scQ(i):
                return AR.sap(SW, F32, [[128, NJ], [1, n]],
                              elem_off=3072 + i * 1024)

            for ci, b1p, a1p, b2p, a2p in ((0, 0, 0, 1, 2), (1, 0, 1, 1, 3),
                                           (2, 2, 0, 3, 2), (3, 2, 1, 3, 3)):
                v.tensor_tensor(scP(0), Pp(buf, b1p, d_, n), Pp(buf, a1p, 0, n),
                                ALU.mult)
                v.tensor_tensor(scP(1), Pp(buf, b2p, d_, n), Pp(buf, a2p, 0, n),
                                ALU.mult)
                v.tensor_tensor(Pp(nb, ci, d_, n), scP(0), scP(1), ALU.add)
            for p in range(4):
                v.tensor_copy(Pp(nb, p, 0, d_), Pp(buf, p, 0, d_))

            def Pb(p):
                return AR.sap(SW, F32, [[0, NJ], [1, n]],
                              elem_off=7680 + buf * 512 + p * KC + d_)
            v.tensor_tensor(scQ(0), Pb(0), Qp(buf, 0, 0, n), ALU.mult)
            v.tensor_tensor(scQ(1), Pb(1), Qp(buf, 1, 0, n), ALU.mult)
            v.tensor_tensor(scQ(0), scQ(0), scQ(1), ALU.add)
            v.tensor_tensor(Qp(nb, 0, d_, n), scQ(0), Qp(buf, 0, d_, n),
                            ALU.add)
            v.tensor_tensor(scQ(0), Pb(2), Qp(buf, 0, 0, n), ALU.mult)
            v.tensor_tensor(scQ(1), Pb(3), Qp(buf, 1, 0, n), ALU.mult)
            v.tensor_tensor(scQ(0), scQ(0), scQ(1), ALU.add)
            v.tensor_tensor(Qp(nb, 1, d_, n), scQ(0), Qp(buf, 1, d_, n),
                            ALU.add)
            v.tensor_copy(Qp(nb, 0, 0, d_), Qp(buf, 0, 0, d_))
            return v.tensor_copy(Qp(nb, 1, 0, d_), Qp(buf, 1, 0, d_))
        P.op("vector", e_hs, inc=("VEC", 1))
        buf = 1 - buf
        d_ *= 2
    pbuf = buf
    pref_vec = P.n("VEC")

    def e_tend(v):
        v.tensor_copy(sw(8704, NJ),
                      AR.sap("prefq", BF16, [[128, NJ], [1, 1]],
                             elem_off=pbuf * 2048 + KC - 1))
        return v.tensor_copy(sw(8712, NJ),
                             AR.sap("prefq", BF16, [[128, NJ], [1, 1]],
                                    elem_off=pbuf * 2048 + 1024 + KC - 1))
    P.op("vector", e_tend, waits=[("VEC", pref_vec)], inc=("VEC", 1))
    P.op("gpsimd", lambda g: g.dma_start(out=tend_b.ap(), in_=sw(8704, 16)),
         waits=[("VEC", P.n("VEC"))], inc=("TB", 16))
    P.op("gpsimd", lambda g: g.collective_compute(
        "AllGather", ALU.bypass,
        replica_groups=[[0, 1], [2, 3], [4, 5], [6, 7]],
        ins=[tend_b.ap().opt()], outs=[tcg_b.ap().opt()]),
        waits=[("TB", P.n("TB"))], inc=("CC", 1))
    P.op("gpsimd", lambda g: g.dma_start(out=sw(8720, 16),
                                         in_=tcg_b.ap()[0:128, :]),
         waits=[("CC", P.n("CC"))], inc=("TB", 16))
    P.op("vector", lambda v: v.tensor_scalar_mul(sw(8720, 16), sw(8720, 16),
                                                 scal(704)),
         waits=[("TB", P.n("TB")), ("SCL", consts_ld)], inc=("VEC", 1))
    tc_vec = P.n("VEC")
    tap("tend", lambda: sw(8704, 16), (128, 16), F32, waits=[("VEC", tc_vec)])
    tap("tc", lambda: sw(8720, 16), (128, 16), F32, waits=[("VEC", tc_vec)])

    def e_t0(v):
        v.tensor_copy(AR.sap(SW, BF16, [[128, NJ], [1, 1]], elem_off=0),
                      AR.sap(SW, F32, [[1, NJ], [1, 1]], elem_off=8720))
        v.tensor_copy(AR.sap(SW, BF16, [[128, NJ], [1, 1]], elem_off=1024),
                      AR.sap(SW, F32, [[1, NJ], [1, 1]], elem_off=8728))
        n = KC - 1

        def Pb(p):
            return AR.sap(SW, F32, [[0, NJ], [1, n]],
                          elem_off=7680 + pbuf * 512 + p * KC)

        def TCc(c):
            return AR.sap(SW, F32, [[1, NJ], [0, n]], elem_off=8720 + c * NJ)

        def scQ(i):
            return AR.sap(SW, F32, [[128, NJ], [1, n]],
                          elem_off=3072 + i * 1024)
        v.tensor_tensor(scQ(0), Pb(0), TCc(0), ALU.mult)
        v.tensor_tensor(scQ(1), Pb(1), TCc(1), ALU.mult)
        v.tensor_tensor(scQ(0), scQ(0), scQ(1), ALU.add)
        v.tensor_tensor(AR.sap(SW, BF16, [[128, NJ], [1, n]], elem_off=1),
                        scQ(0), Qp(pbuf, 0, 0, n), ALU.add)
        v.tensor_tensor(scQ(0), Pb(2), TCc(0), ALU.mult)
        v.tensor_tensor(scQ(1), Pb(3), TCc(1), ALU.mult)
        v.tensor_tensor(scQ(0), scQ(0), scQ(1), ALU.add)
        return v.tensor_tensor(AR.sap(SW, BF16, [[128, NJ], [1, n]],
                                      elem_off=1024 + 1),
                               scQ(0), Qp(pbuf, 1, 0, n), ALU.add)
    P.op("vector", e_t0, waits=[("VEC", tc_vec)], inc=("VEC", 1))
    t0_vec = P.n("VEC")
    tap("t0f", lambda: stb(0), (128, 1024), BF16, waits=[("VEC", t0_vec)])

    emit_pass(True, first_waits=[("VEC", t0_vec)])
    w_vec = P.n("VEC")
    tap("w", lambda: AR.ap("C", BF16, (NJ * TOK,)), (128, NJ * TOK), BF16,
        waits=[("VEC", w_vec)])
    scan_done = [("VEC", w_vec)]

    AR.release("scanws", "prefq", "fdec", "sdec", "bdec")
    AR.alloc("statrows3", 16 * 1024)
    AR.alloc("xn1rl", NJ * BLK * 2)
    AR.alloc("qrl", NJ * BLK * 2)
    AR.alloc("gstg", NJ * BLK * 2)
    AR.alloc("og", NJ * BLK * 2)
    AR.alloc("yb", 2 * NJ * BLK * 2)
    AR.alloc("lnt3", NJ * BLK * 4)
    AR.alloc("x1b", NJ * BLK * 4)
    AR.alloc("xsq3", NJ * BLK * 2)
    RN3 = ("xsq3", "lnt3", "statrows3")

    def wgt(jo, kt):
        return AR.ap("wgbuf", BF16, (NJ * NJ, 128))[:, jo * NJ + kt]

    def wot(jo, kt):
        return AR.ap("wobuf", BF16, (NJ * NJ, 128))[:, jo * NJ + kt]

    prev = {"og": 0, "gact": 0, "yact": 0, "ymm": 0, "wgmm": 0,
            "xn2sd": [0, 0], "x1sd": 0, "xn2va": [0, 0]}
    for blk in range(NBLK):
        pb = blk % 2
        P.op("sync", lambda s, blk=blk: s.dma_start(
            out=AR.ap("xn1rl", BF16, (NJ, BLK)),
            in_=AP(xn1sp.tensor, blk * BLK,
                   [[TOK, 128], [128 * TOK, NJ], [1, BLK]])),
            waits=[("XN1S", xn1sp_sd), ("MM", prev["wgmm"])]
            + (scan_done if blk == 0 else []), inc=("XN1L", 16))
        ldxn = P.n("XN1L")
        P.op("sync", lambda s, blk=blk: s.dma_start(
            out=AR.ap("qrl", BF16, (NJ, BLK)),
            in_=AP(qsp.tensor, blk * BLK,
                   [[TOK, 128], [128 * TOK, NJ], [1, BLK]])),
            waits=q_sd + [("VEC", prev["og"])], inc=("QRL", 16))
        ldq = P.n("QRL")

        for jo in range(NJ):
            slot = (blk * NJ + jo) % 3

            def e_mg(t, jo=jo, slot=slot):
                last = None
                for kt in range(NJ):
                    last = t.matmul(ps[slot][:, :], wgt(jo, kt),
                                    AR.ap("xn1rl", BF16, (NJ, BLK))[:, kt],
                                    start=(kt == 0), stop=(kt == NJ - 1))
                return last
            P.op("tensor", e_mg, waits=[rg, ("XN1L", ldxn)]
                 + slot_free[slot], inc=("MM", 1))
            P.op("scalar", lambda s, jo=jo, slot=slot: s.activation(
                AR.ap("gstg", BF16, (NJ, BLK))[:, jo], ps[slot][:, :],
                ACT.Sigmoid),
                waits=[("MM", P.n("MM")), ("VEC", prev["og"])]
                + (scan_done if blk == 0 else []), inc=("ACT", 1))
            slot_free[slot] = [("ACT", P.n("ACT"))]
        g_act = P.n("ACT")
        prev["wgmm"] = P.n("MM")

        def e_og(v, blk=blk, pb=pb):
            last = None
            for jo in range(NJ):
                v.tensor_tensor(
                    AR.sap("og", BF16, [[LC, 16], [1, LC]],
                           elem_off=jo * BLK),
                    AR.sap("qrl", BF16, [[LC, 16], [1, LC]], elem_off=jo * BLK),
                    AR.sap("C", BF16, [[1, 16], [1024, LC]],
                           elem_off=jo * 128 + blk * 16),
                    ALU.mult)
                last = v.scalar_tensor_tensor(
                    AR.ap("og", BF16, (NJ, BLK))[:, jo],
                    AR.ap("og", BF16, (NJ, BLK))[:, jo], 0.5,
                    AR.ap("gstg", BF16, (NJ, BLK))[:, jo], ALU.mult, ALU.mult)
            return last
        P.op("vector", e_og,
             waits=[("QRL", ldq), ("ACT", g_act), ("XN2S", prev["xn2sd"][0])]
             + scan_done, inc=("VEC", 1))
        og_vec = P.n("VEC")
        prev["og"] = og_vec

        for jo in range(NJ):
            slot = 3 + (blk * NJ + jo) % 3

            def e_my(t, jo=jo, slot=slot, pb=pb):
                last = None
                for kt in range(NJ):
                    last = t.matmul(ps[slot][:, :], wot(jo, kt),
                                    AR.ap("og", BF16, (NJ, BLK))[:, kt],
                                    start=(kt == 0), stop=(kt == NJ - 1))
                return last
            P.op("tensor", e_my, waits=[ro, ("VEC", og_vec)]
                 + slot_free[slot], inc=("MM", 1))
            P.op("scalar", lambda s, jo=jo, slot=slot, pb=pb: s.activation(
                AR.ap("yb", BF16, (2, NJ, BLK))[:, pb, jo], ps[slot][:, :],
                ACT.Identity),
                waits=[("MM", P.n("MM")), ("VEC", prev["xn2va"][pb])]
                + (scan_done if blk == 0 else []), inc=("ACT", 1))
            slot_free[slot] = [("ACT", P.n("ACT"))]
        y_act = P.n("ACT")
        prev["ymm"] = P.n("MM")

        va = emit_ln3(lambda pb=pb: AR.ap("yb", BF16, (2, NJ, BLK))[:, pb],
                      lambda pb=pb: AR.ap("yb", BF16, (2, NJ, BLK))[:, pb],
                      st_slot=blk % 3, bank=blk % 2,
                      src_ready=[("ACT", y_act)], rn=RN3)
        P.op("sync", lambda s, blk=blk: s.dma_start(
            out=AR.ap("lnt3", F32, (NJ, BLK)),
            in_=AP(x_hbm.tensor, blk * BLK,
                   [[TOK, 128], [128 * TOK, NJ], [1, BLK]])),
            waits=[("VEC", va)], inc=("XLD", 16))
        ldx = P.n("XLD")
        P.op("vector", lambda v, pb=pb: v.tensor_tensor(
            AR.ap("x1b", F32, (NJ, BLK)), AR.ap("lnt3", F32, (NJ, BLK)),
            AR.ap("yb", BF16, (2, NJ, BLK))[:, pb], ALU.add),
            waits=[("XLD", ldx), ("X1S", prev["x1sd"])], inc=("VEC", 1))
        x1_vec = P.n("VEC")
        P.op("scalar", lambda s, blk=blk: s.dma_start(
            out=AP(x1sp.tensor, blk * BLK,
                   [[TOK, 128], [128 * TOK, NJ], [1, BLK]]),
            in_=AR.ap("x1b", F32, (NJ, BLK))), waits=[("VEC", x1_vec)],
            inc=("X1S", 16))
        prev["x1sd"] = P.n("X1S")
        P.op("vector", lambda v, pb=pb: v.tensor_copy(
            AR.ap("yb", BF16, (2, NJ, BLK))[:, pb],
            AR.ap("x1b", F32, (NJ, BLK))),
            inc=("VEC", 1))
        cp_vec = P.n("VEC")
        va2 = emit_ln3(lambda pb=pb: AR.ap("yb", BF16, (2, NJ, BLK))[:, pb],
                       lambda: AR.ap("og", BF16, (NJ, BLK)),
                       st_slot=3 + (blk % 3), bank=blk % 2,
                       src_ready=[("VEC", cp_vec), ("MM", prev["ymm"])],
                       rn=RN3)
        prev["xn2va"][pb] = va2
        P.op("scalar", lambda s, blk=blk, pb=pb: s.dma_start(
            out=AP(xn2sp.tensor, blk * BLK,
                   [[TOK, 128], [128 * TOK, NJ], [1, BLK]]),
            in_=AR.ap("og", BF16, (NJ, BLK))), waits=[("VEC", va2)],
            inc=("XN2S", 16))
        prev["xn2sd"][0] = P.n("XN2S")
    xn2_sd = P.n("XN2S")
    x1_sd = P.n("X1S")

    p3_done = P.now()
    AR.release("C", "wgbuf", "wobuf", "xn1rl", "qrl", "gstg", "og", "yb",
               "lnt3", "x1b", "xsq3", "statrows3")
    AR.alloc("B2", 64 * 1024)
    AR.alloc("fw", 4 * NKT * 128 * 2)
    AR.alloc("fstg", 4 * BLK * 2)

    P.op("sync", lambda s: s.dma_start(
        out=AR.ap("B2", BF16, (NJ, TOK)),
        in_=AP(xn2sp.tensor, 0, [[TOK, 128], [128 * TOK, NJ], [1, TOK]])),
        waits=[("XN2S", xn2_sd)] + p3_done, inc=("B2L", 16))
    xn2_ld = P.n("B2L")

    fw_free = [p3_done, p3_done, p3_done, p3_done]

    def ld_fcol(widx, wsrc, fo, nkt, waits=()):
        fsem = f"FW{widx}"

        def e(g, widx=widx, fo=fo, nkt=nkt, wsrc=wsrc):
            return g.dma_start(
                out=AR.ap("fw", BF16, (4, NKT, 128))[:, widx, 0:nkt],
                in_=AP(wsrc.tensor, fo * nkt * 128 * 128,
                       [[128, 128], [128 * 128, nkt], [1, 128]]))
        P.op("gpsimd", e, waits=list(waits) + fw_free[widx], inc=(fsem, 16))
        return (fsem, P.n(fsem))

    def fct(widx, kt):
        return AR.ap("fw", BF16, (4, NKT, 128))[:, widx, kt]

    stg_free = [p3_done, p3_done, p3_done, p3_done]
    for fo in range(NFO):
        wa = (fo % 2) * 2
        wb2 = wa + 1
        r1 = ld_fcol(wa, w_1, fo, NJ)
        r2 = ld_fcol(wb2, w_2, fo, NJ)
        m1l = m2l = 0
        for blk in range(NBLK):
            s1v = blk % 4
            s2v = 4 + blk % 4

            def e_m1(t, blk=blk, wa=wa, slot=s1v):
                last = None
                for kt in range(NJ):
                    last = t.matmul(
                        ps[slot][:, :], fct(wa, kt),
                        AR.ap("B2", BF16, (NJ, TOK))[:, kt,
                                                     blk * BLK:(blk + 1) * BLK],
                        start=(kt == 0), stop=(kt == NJ - 1))
                return last
            P.op("tensor", e_m1, waits=[r1, ("B2L", xn2_ld)]
                 + slot_free[s1v], inc=("MM", 1))
            m1l = P.n("MM")

            def e_m2(t, blk=blk, wb2=wb2, slot=s2v):
                last = None
                for kt in range(NJ):
                    last = t.matmul(
                        ps[slot][:, :], fct(wb2, kt),
                        AR.ap("B2", BF16, (NJ, TOK))[:, kt,
                                                     blk * BLK:(blk + 1) * BLK],
                        start=(kt == 0), stop=(kt == NJ - 1))
                return last
            P.op("tensor", e_m2, waits=[r2] + slot_free[s2v],
                 inc=("MM", 1))
            m2l = P.n("MM")
            stg = blk % 4
            P.op("scalar", lambda s, fo=fo, slot=s1v, stg=stg: s.activation(
                AR.ap("fstg", BF16, (4, BLK))[:, stg], ps[slot][:, :], ACT.Silu,
                scale=scal(0, 32)[:, fo:fo + 1]),
                waits=[("MM", m1l)] + stg_free[stg], inc=("ACT", 1))
            a1v = P.n("ACT")
            slot_free[s1v] = [("ACT", a1v)]
            P.op("vector", lambda v, fo=fo, slot=s2v, stg=stg:
                 v.scalar_tensor_tensor(
                     AR.ap("fstg", BF16, (4, BLK))[:, stg], ps[slot][:, :],
                     scal(128, 32)[:, fo:fo + 1],
                     AR.ap("fstg", BF16, (4, BLK))[:, stg], ALU.mult, ALU.mult),
                 waits=[("MM", m2l), ("ACT", a1v)], inc=("VEC", 1))
            v2v = P.n("VEC")
            slot_free[s2v] = [("VEC", v2v)]
            P.op("scalar", lambda s, fo=fo, blk=blk, stg=stg: s.dma_start(
                out=AP(g1sp.tensor, fo * 128 * TOK + blk * BLK,
                       [[TOK, 128], [1, BLK]]),
                in_=AR.ap("fstg", BF16, (4, BLK))[:, stg]),
                waits=[("VEC", v2v)], inc=(f"FS{stg}", 16))
            stg_free[stg] = [(f"FS{stg}", P.n(f"FS{stg}"))]
        fw_free[wa] = [("MM", m1l)]
        fw_free[wb2] = [("MM", m2l)]
    g1_sd = [(f"FS{i}", P.n(f"FS{i}")) for i in range(4)]

    p4_done = P.now()
    AR.release("B2")
    AR.alloc("rhsA", 64 * 1024)
    AR.alloc("rhsB", 64 * 1024)

    def h_layer2(src_sp, dst_sp, wsrc, scale_off, src_sd_per_sup, extra0):
        mm_after = {}
        caps = []
        ld_done = {}
        for sup in range(NSUP):
            reg = "rhsA" if sup % 2 == 0 else "rhsB"
            rsem = "RHA" if sup % 2 == 0 else "RHB"
            w0 = list(src_sd_per_sup[sup]) + list(extra0)
            w0.append(("MM", mm_after.get(sup - 2, P.n("MM"))))
            P.op("sync", lambda s, sup=sup, src_sp=src_sp, reg=reg: s.dma_start(
                out=AR.ap(reg, BF16, (2, 16, SUP))[:, 0],
                in_=AP(src_sp.tensor, sup * SUP,
                       [[TOK, 128], [128 * TOK, 16], [1, SUP]])),
                waits=w0, inc=(rsem, 16))
            P.op("scalar", lambda s, sup=sup, src_sp=src_sp, reg=reg: s.dma_start(
                out=AR.ap(reg, BF16, (2, 16, SUP))[:, 1],
                in_=AP(src_sp.tensor, 16 * 128 * TOK + sup * SUP,
                       [[TOK, 128], [128 * TOK, 16], [1, SUP]])),
                waits=w0, inc=(rsem, 16))
            ld_done[sup] = (rsem, P.n(rsem))

            def rhs_t(kt, c0, reg=reg):
                return AR.ap(reg, BF16, (2, 16, SUP))[:, kt // 16, kt % 16,
                                                     c0:c0 + BLK]

            for fo in range(NFO):
                widx = fo % 2
                rw = ld_fcol(widx, wsrc, fo, NKT)
                mml = 0
                for sb in range(SUP // BLK):
                    slot = (fo * (SUP // BLK) + sb) % 8

                    def e_mm(t, sb=sb, widx=widx, slot=slot, rhs_t=rhs_t):
                        last = None
                        for kt in range(NKT):
                            last = t.matmul(ps[slot][:, :], fct(widx, kt),
                                            rhs_t(kt, sb * BLK),
                                            start=(kt == 0),
                                            stop=(kt == NKT - 1))
                        return last
                    P.op("tensor", e_mm, waits=[rw, ld_done[sup]]
                         + slot_free[slot], inc=("MM", 1))
                    mml = P.n("MM")
                    stg = sb % 4 if SUP // BLK >= 4 else (fo * 2 + sb) % 4
                    P.op("scalar", lambda s, fo=fo, slot=slot, stg=stg,
                         so=scale_off: s.activation(
                             AR.ap("fstg", BF16, (4, BLK))[:, stg],
                             ps[slot][:, :], ACT.Gelu,
                             scale=scal(so, 32)[:, fo:fo + 1]),
                         waits=[("MM", mml)] + stg_free[stg], inc=("ACT", 1))
                    ev = P.n("ACT")
                    slot_free[slot] = [("ACT", ev)]
                    P.op("scalar", lambda s, fo=fo, sup=sup, sb=sb, stg=stg,
                         dst_sp=dst_sp: s.dma_start(
                             out=AP(dst_sp.tensor,
                                    fo * 128 * TOK + sup * SUP + sb * BLK,
                                    [[TOK, 128], [1, BLK]]),
                             in_=AR.ap("fstg", BF16, (4, BLK))[:, stg]),
                         waits=[("ACT", ev)], inc=(f"FS{stg}", 16))
                    stg_free[stg] = [(f"FS{stg}", P.n(f"FS{stg}"))]
                fw_free[widx] = [("MM", mml)]
            mm_after[sup] = P.n("MM")
            caps.append([(f"FS{i}", P.n(f"FS{i}")) for i in range(4)])
        return caps

    h1_caps = h_layer2(g1sp, h1sp, h_1, 256, [g1_sd] * NSUP, p4_done)
    h2_caps = h_layer2(h1sp, h2sp, h_2, 384, h1_caps, [])
    h2_sd = h2_caps[-1]

    p6_done = P.now()
    AR.release("rhsB")
    AR.alloc("statrows7", 16 * 1024)
    AR.alloc("w3s", 2 * 2 * NJ * BLK * 2)
    AR.alloc("xsq7", NJ * BLK * 2)
    AR.alloc("lnt7", NJ * BLK * 4)
    AR.alloc("x1rl", NJ * BLK * 4)
    RN7 = ("xsq7", "lnt7", "statrows7")

    prev7 = {"out_vec": 0, "out_st": [0, 0], "w3used": [[0, 0], [0, 0]],
             "mmafter": {}}
    for sup in range(NS3):
        sp = sup % 2
        reg7 = "rhsA"
        P.op("sync", lambda s, sup=sup, reg7=reg7: s.dma_start(
            out=AR.ap(reg7, BF16, (NKT, S3)),
            in_=AP(h2sp.tensor, sup * S3,
                   [[TOK, 128], [128 * TOK, NKT], [1, S3]])),
            waits=list(h2_caps[sup])
            + [("MM", prev7["mmafter"].get(sup - 1, P.n("MM")))],
            inc=("RHA", 16))
        ldr = [("RHA", P.n("RHA"))]
        mm_done = []
        for jo in range(NJ):
            widx = 2 + jo % 2
            rw = ld_fcol(widx, w_3, jo, NKT, waits=ldr)
            for sb in range(S3 // BLK):
                slot = (jo * 2 + sb) % 6

                def e_mm(t, jo=jo, sb=sb, widx=widx, slot=slot, reg7=reg7):
                    last = None
                    for kt in range(NKT):
                        last = t.matmul(
                            ps[slot][:, :], fct(widx, kt),
                            AR.ap(reg7, BF16, (NKT, S3))[:, kt,
                                                         sb * BLK:(sb + 1) * BLK],
                            start=(kt == 0), stop=(kt == NKT - 1))
                    return last
                P.op("tensor", e_mm, waits=[rw] + ldr
                     + slot_free[slot], inc=("MM", 1))
                mml = P.n("MM")
                P.op("vector", lambda v, jo=jo, sb=sb, slot=slot, sp=sp:
                     v.tensor_scalar_mul(
                         AR.ap("w3s", BF16, (2, 2, NJ, BLK))[:, sp, sb, jo],
                         ps[slot][:, :], scal(512, 8)[:, jo:jo + 1]),
                     waits=[("MM", mml), ("VEC", prev7["w3used"][sp][sb])]
                     + (p6_done if sup == 0 and jo == 0 else []),
                     inc=("VEC", 1))
                slot_free[slot] = [("VEC", P.n("VEC"))]
                fw_free[widx] = [("MM", mml)]
        w3s_vec = P.n("VEC")
        prev7["mmafter"][sup] = P.n("MM")
        for sb in range(S3 // BLK):
            blk = sup * (S3 // BLK) + sb
            va = emit_ln3(
                lambda sb=sb, sp=sp: AR.ap("w3s", BF16, (2, 2, NJ, BLK))[:, sp, sb],
                lambda sb=sb, sp=sp: AR.ap("w3s", BF16, (2, 2, NJ, BLK))[:, sp, sb],
                st_slot=blk % 3, bank=blk % 2,
                src_ready=[("VEC", w3s_vec)]
                + (p6_done if blk == 0 else []), rn=RN7)
            P.op("sync", lambda s, blk=blk, sb=sb: s.dma_start(
                out=AR.ap("x1rl", F32, (NJ, BLK)),
                in_=AP(x1sp.tensor, blk * BLK,
                       [[TOK, 128], [128 * TOK, NJ], [1, BLK]])),
                waits=[("X1S", x1_sd), ("OUT", max(prev7["out_st"]))]
                + (p6_done if blk == 0 else []), inc=("X1R", 16))
            ldx1 = P.n("X1R")
            P.op("vector", lambda v, sb=sb, sp=sp: v.tensor_tensor(
                AR.ap("x1rl", F32, (NJ, BLK)),
                AR.ap("x1rl", F32, (NJ, BLK)),
                AR.ap("w3s", BF16, (2, 2, NJ, BLK))[:, sp, sb], ALU.add),
                waits=[("X1R", ldx1), ("VEC", va)],
                inc=("VEC", 1))
            ov = P.n("VEC")
            prev7["out_vec"] = ov
            prev7["w3used"][sp][sb] = ov
            P.op("scalar", lambda s, blk=blk, sb=sb: s.dma_start(
                out=AP(out_hbm.tensor, blk * BLK,
                       [[TOK, 128], [128 * TOK, NJ], [1, BLK]]),
                in_=AR.ap("x1rl", F32, (NJ, BLK))),
                waits=[("VEC", ov)], inc=("OUT", 16))
            prev7["out_st"][sb] = P.n("OUT")

    P.op("gpsimd", lambda g: g.nop(),
         waits=[("OUT", P.n("OUT")), ("TAP", P.n("TAP"))], inc=("GP", 1))

    _DRAIN_OPS = frozenset((
        "tensor_tensor", "tensor_add", "tensor_sub", "tensor_mul",
        "tensor_max", "tensor_copy", "tensor_scalar", "tensor_scalar_mul",
        "tensor_scalar_add", "tensor_single_scalar", "scalar_tensor_tensor",
        "tensor_tensor_scan", "reciprocal", "memset", "activation",
        "square", "copy", "mul", "add", "tensor_relu", "select"))

    class _Draining:
        def __init__(self, e):
            self._e = e

        def __getattr__(self, name):
            f = getattr(self._e, name)
            if name in _DRAIN_OPS:
                def g(*a, **k):
                    inst = f(*a, **k)
                    self._e.drain()
                    return inst
                return g
            return f

    with nc.Block() as block:
        def mk(eng_name, wrap):
            def run(e):
                if wrap:
                    e = _Draining(e)
                for emit, waits, inc in P.ops[eng_name]:
                    for s, v in waits:
                        e.wait_ge(sems[s], v)
                    inst = emit(e)
                    if inc is not None and inst is not None:
                        inst.then_inc(sems[inc[0]], inc[1])
            return run
        block.sync(mk("sync", False))
        block.gpsimd(mk("gpsimd", True))
        block.vector(mk("vector", True))
        block.scalar(mk("scalar", True))
        block.tensor(mk("tensor", False))

    nc._keepalive = es
    return nc, tap_d


def _perm():
    idx = np.empty(D, np.int64)
    for j in range(NJ):
        for p in range(128):
            idx[j * 128 + p] = (p // 8) * 64 + 8 * j + (p % 8)
    return idx


def _tile(MT, njo, nkt):
    I, O = MT.shape
    assert I == nkt * 128 and O == njo * 128, (MT.shape, njo, nkt)
    return np.ascontiguousarray(
        MT.reshape(nkt, 128, njo, 128).transpose(2, 0, 1, 3)).astype(np.float32)


def kernel(**inputs):
    x = np.asarray(inputs["L0_attn_x"], np.float32)
    perm = _perm()
    rep = np.arange(128) // 8

    def W(name):
        return np.asarray(inputs[name], np.float32)

    shared = dict(
        wq=_tile(W("L0_attn_Wq").T[:, perm], NJ, NJ),
        wk=_tile(W("L0_attn_Wk").T[:, perm], NJ, NJ),
        wv=_tile(W("L0_attn_Wv").T[:, perm], NJ, NJ),
        wg=_tile(W("L0_attn_Wg").T[:, perm], NJ, NJ),
        wo=_tile(np.ascontiguousarray(W("L0_attn_Wo").T[perm, :]), NJ, NJ),
        wb=_tile(np.ascontiguousarray(W("L0_attn_Wb").T[:, rep]), 1, NJ),
        wfd=_tile(np.ascontiguousarray(W("L0_attn_Wfd").T[:, rep]), 1, NJ),
        wsd=_tile(np.ascontiguousarray(W("L0_attn_Wsd").T[:, rep]), 1, NJ),
        bfd=np.ascontiguousarray(W("L0_attn_bfd")[rep].reshape(128, 1)),
        bsd=np.ascontiguousarray(W("L0_attn_bsd")[rep].reshape(128, 1)),
        w1=_tile(W("L0_ffn_w1_w").T, NFO, NJ),
        w2=_tile(W("L0_ffn_w2_w").T, NFO, NJ),
        h1=_tile(W("L0_ffn_h1_w").T, NFO, NKT),
        h2=_tile(W("L0_ffn_h2_w").T, NFO, NKT),
        w3=_tile(W("L0_ffn_w3_w").T, NJ, NKT),
        s1=np.ascontiguousarray(W("L0_ffn_w1_s").reshape(NFO, 128).T),
        s2=np.ascontiguousarray(W("L0_ffn_w2_s").reshape(NFO, 128).T),
        sh1=np.ascontiguousarray(W("L0_ffn_h1_s").reshape(NFO, 128).T),
        sh2=np.ascontiguousarray(W("L0_ffn_h2_s").reshape(NFO, 128).T),
        s3=np.ascontiguousarray(W("L0_ffn_w3_s").reshape(NJ, 128).T),
    )

    nc, _ = build_nc(DEBUG_TAPS)
    in_maps = []
    for c in range(8):
        b, half = c // 2, c % 2
        in_maps.append({
            **shared,
            "x": np.ascontiguousarray(x[b, half * TOK:(half + 1) * TOK, :].T),
            "mask": np.full((128, 1), float(half), np.float32),
        })

    res = run_bass_kernel_spmd(nc, in_maps, core_ids=list(range(8)))
    out = np.empty((4, 8192, 1024), np.float32)
    for c in range(8):
        b, half = c // 2, c % 2
        out[b, half * TOK:(half + 1) * TOK, :] = res.results[c]["out"].T
    return out


if __name__ == "__main__":
    nc, _ = build_nc()
    ins = {e: len(ops) for e, ops in []} if False else None
    print("built ok")


# version 13 (final)
# speedup vs baseline: 1.0088x; 1.0088x over previous
import numpy as np
import concourse.bass as bass
import concourse.mybir as mybir
from concourse.ap import AP
from concourse.bass_utils import run_bass_kernel_spmd

F32 = mybir.dt.float32
BF16 = mybir.dt.bfloat16
ACT = mybir.ActivationFunctionType
ALU = mybir.AluOpType

TOK = 4096
NBLK = 8
BLK = 512
D = 1024
NJ = 8
NFO = 32
NKT = 32
KC = 128
LC = 32
MIX = 0.05
NSUP = 4
SUP = TOK // NSUP
NS3 = 4
S3 = TOK // NS3

DEBUG_TAPS = ()


class Prog:
    ENGINES = ("sync", "gpsimd", "vector", "scalar", "tensor")

    def __init__(self):
        self.ops = {e: [] for e in self.ENGINES}
        self.cnt = {}
        self.last_wait = {}

    def n(self, sem):
        return self.cnt.get(sem, 0)

    def now(self):
        return [(s, v) for s, v in self.cnt.items() if s != "CC"]

    def op(self, eng, emit, waits=(), inc=None):
        w = []
        for s, v in waits:
            if v <= 0:
                continue
            k = (eng, s)
            if self.last_wait.get(k, -1) >= v:
                continue
            self.last_wait[k] = v
            w.append((s, v))
        if inc is not None:
            s, a = inc
            self.cnt[s] = self.cnt.get(s, 0) + a
        self.ops[eng].append((emit, w, inc))


class Arena:
    def __init__(self, size):
        self.size = size
        self.free = [(0, size)]
        self.tensor = None
        self.named = {}
        self.hw = 0

    def alloc(self, name, nbytes, align=64):
        assert name not in self.named, f"region name reused: {name}"
        nbytes = (nbytes + align - 1) // align * align
        for i, (off, ln) in enumerate(self.free):
            aoff = (off + align - 1) // align * align
            if aoff + nbytes <= off + ln:
                repl = []
                if aoff - off:
                    repl.append((off, aoff - off))
                if off + ln - (aoff + nbytes):
                    repl.append((aoff + nbytes, off + ln - (aoff + nbytes)))
                self.free[i:i + 1] = repl
                self.named[name] = (aoff, nbytes)
                self.hw = max(self.hw, self.size - sum(l for _, l in self.free))
                return aoff
        raise MemoryError(f"arena OOM for {name} ({nbytes}B) free={self.free}")

    def release(self, *names):
        for name in names:
            off, ln = self.named[name]
            self.released = getattr(self, "released", set()) | {name}
            self.free.append((off, ln))
        self.free.sort()
        merged = []
        for off, ln in self.free:
            if merged and merged[-1][0] + merged[-1][1] == off:
                merged[-1] = [merged[-1][0], merged[-1][1] + ln]
            else:
                merged.append([off, ln])
        self.free = [tuple(x) for x in merged]

    def base(self, dtype):
        return self.tensor[:].bitcast(dtype)

    def ap(self, name, dtype, shape, elem_off=0, part=128):
        off, _ = self.named[name]
        es = mybir.dt.size(dtype)
        b = self.base(dtype)
        dims = [[b.ap[0][0], part]]
        stride = 1
        rev = []
        for s in reversed(shape):
            rev.append([stride, s])
            stride *= s
        dims += rev[::-1]
        return AP(b.tensor, b.offset + off // es + elem_off, dims)

    def sap(self, name, dtype, dims, elem_off=0, part=128):
        off, _ = self.named[name]
        es = mybir.dt.size(dtype)
        b = self.base(dtype)
        return AP(b.tensor, b.offset + off // es + elem_off,
                  [[b.ap[0][0], part]] + [list(d) for d in dims])


def build_nc(taps=()):
    nc = bass.Bass("TRN2", target_bir_lowering=False, debug=False)
    taps = set(taps)

    def dram_in(name, shape, dtype=F32):
        return nc.dram_tensor(name, list(shape), dtype, kind="ExternalInput").ap()

    x_hbm = dram_in("x", [D, TOK])
    w_q = dram_in("wq", [NJ, NJ, 128, 128])
    w_k = dram_in("wk", [NJ, NJ, 128, 128])
    w_v = dram_in("wv", [NJ, NJ, 128, 128])
    w_g = dram_in("wg", [NJ, NJ, 128, 128])
    w_o = dram_in("wo", [NJ, NJ, 128, 128])
    w_b = dram_in("wb", [1, NJ, 128, 128])
    w_fd = dram_in("wfd", [1, NJ, 128, 128])
    w_sd = dram_in("wsd", [1, NJ, 128, 128])
    bfd_r = dram_in("bfd", [128, 1])
    bsd_r = dram_in("bsd", [128, 1])
    mask_in = dram_in("mask", [128, 1])
    w_1 = dram_in("w1", [NFO, NJ, 128, 128])
    w_2 = dram_in("w2", [NFO, NJ, 128, 128])
    h_1 = dram_in("h1", [NFO, NKT, 128, 128])
    h_2 = dram_in("h2", [NFO, NKT, 128, 128])
    w_3 = dram_in("w3", [NJ, NKT, 128, 128])
    s_1 = dram_in("s1", [128, NFO])
    s_2 = dram_in("s2", [128, NFO])
    s_h1 = dram_in("sh1", [128, NFO])
    s_h2 = dram_in("sh2", [128, NFO])
    s_3 = dram_in("s3", [128, NJ])

    out_hbm = nc.dram_tensor("out", [D, TOK], F32, kind="ExternalOutput").ap()

    qsp = nc.dram_tensor("qsp", [NJ, 128, TOK], BF16).ap()
    xn1sp = nc.dram_tensor("xn1sp", [NJ, 128, TOK], BF16).ap()
    xn2sp = nc.dram_tensor("xn2sp", [NJ, 128, TOK], BF16).ap()
    x1sp = nc.dram_tensor("x1sp", [NJ, 128, TOK], F32).ap()
    g1sp = nc.dram_tensor("g1sp", [NFO, 128, TOK], BF16).ap()
    h1sp = nc.dram_tensor("h1sp", [NFO, 128, TOK], BF16).ap()
    h2sp = nc.dram_tensor("h2sp", [NFO, 128, TOK], BF16).ap()
    tend_b = nc.dram_tensor("tend_b", [128, 16], F32)
    tcg_b = nc.dram_tensor("tcg_b", [256, 16], F32)

    P = Prog()
    AR = Arena(206 * 1024)
    tap_d = {}

    from contextlib import ExitStack
    es = ExitStack()
    arena_t = es.enter_context(
        nc.sbuf_tensor("arena", [128, AR.size], mybir.dt.uint8))
    AR.tensor = arena_t
    ps = [es.enter_context(nc.psum_tensor(f"ps{i}", [128, 512], F32))
          for i in range(8)]
    sems = {}
    for s in ("MM", "ACT", "VEC", "GP", "CC",
              "XB0", "XB1", "WQA", "WQB", "WG", "WO",
              "FW0", "FW1", "FW2", "FW3", "SCL", "TB",
              "XN1L", "QRL", "XLD", "B2L", "RHA", "RHB", "X1R",
              "XN1S", "X1S", "XN2S", "OUT", "TAP",
              "FS0", "FS1", "FS2", "FS3",
              "QS0", "QS1", "QS2", "QS3", "QS4", "QS5", "QS6", "QS7"):
        sems[s] = es.enter_context(nc.semaphore(s))

    slot_free = [[] for _ in range(8)]

    def tap(name, ap_fn, shape, dtype=F32, waits=()):
        if name not in taps:
            return
        hd = nc.dram_tensor("tap_" + name, list(shape), dtype,
                            kind="ExternalOutput")
        tap_d[name] = hd
        P.op("scalar", lambda s, hd=hd, ap_fn=ap_fn:
             s.dma_start(out=hd.ap(), in_=ap_fn()),
             waits=waits, inc=("TAP", 16))

    AR.alloc("consts", 1024)
    AR.alloc("scales", 4096)
    AR.alloc("statrows", 16 * 1024)
    ones_col = AR.ap("consts", BF16, (1,), elem_off=0)
    ones_row = AR.ap("consts", BF16, (128,), elem_off=64, part=1)
    ones_row_f = AR.ap("consts", F32, (128,), elem_off=96, part=1)

    def scal(off_b, n=1):
        return AR.ap("scales", F32, (n,), elem_off=off_b // 4)

    eps_row = AR.ap("consts", F32, (1,), elem_off=232, part=1)
    neghalf = AR.ap("consts", F32, (1,), elem_off=233, part=1)
    mixc = AR.ap("consts", F32, (1,), elem_off=234)
    mix2c = AR.ap("consts", F32, (1,), elem_off=235)
    P.op("vector", lambda v: (v.memset(ones_col, 1.0),
                              v.memset(ones_row, 1.0),
                              v.memset(ones_row_f, 1.0),
                              v.memset(eps_row, 1e-5),
                              v.memset(neghalf, -0.5),
                              v.memset(mixc, MIX),
                              v.memset(mix2c, MIX * MIX))[-1], inc=("VEC", 1))
    consts_vec = P.n("VEC")
    for off_b, src, n in ((0, s_1, 32), (128, s_2, 32), (256, s_h1, 32),
                          (384, s_h2, 32), (512, s_3, 8), (640, bfd_r, 1),
                          (672, bsd_r, 1), (704, mask_in, 1)):
        P.op("sync", lambda s, o=off_b, sr=src, n=n:
             s.dma_start(out=scal(o, n), in_=sr), inc=("SCL", 16))
    consts_ld = P.n("SCL")

    ln_state = {"bc_mm": [0, 0], "stat_mm": [0, 0]}

    def emit_ln3(src3, dst3, st_slot, bank, src_ready,
                 rn=("xsq", "lnt", "statrows")):
        xsq_nm, lnt_nm, sr_nm = rn
        st2 = (st_slot + 3) % 6
        ro = 1024 * bank

        def row(off):
            return AR.ap(sr_nm, F32, (BLK,), elem_off=ro + off, part=1)

        def rowb(off):
            return AR.ap(sr_nm, BF16, (BLK,), elem_off=6144 + ro + off,
                         part=1)

        tmp = AR.ap(sr_nm, F32, (BLK,), elem_off=2048 + 512 * bank, part=1)

        sq_guard = [("MM", ln_state["stat_mm"][bank])]
        for j in range(NJ):
            P.op("scalar", lambda s, j=j, src3=src3, xsq_nm=xsq_nm:
                 s.activation(AR.ap(xsq_nm, BF16, (NJ, BLK))[:, j],
                              src3()[:, j], ACT.Square),
                waits=list(src_ready) + sq_guard, inc=("ACT", 1))
        sq_act = P.n("ACT")

        def e_mm(t, src3=src3, xsq_nm=xsq_nm):
            last = None
            src = src3()
            for j in range(NJ):
                last = t.matmul(ps[st_slot][0:1, :], ones_col, src[:, j],
                                start=(j == 0), stop=(j == NJ - 1))
            for j in range(NJ):
                last = t.matmul(ps[st2][0:1, :], ones_col,
                                AR.ap(xsq_nm, BF16, (NJ, BLK))[:, j],
                                start=(j == 0), stop=(j == NJ - 1))
            return last
        P.op("tensor", e_mm,
             waits=list(src_ready) + [("ACT", sq_act), ("VEC", consts_vec)]
             + slot_free[st_slot] + slot_free[st2], inc=("MM", 1))
        mm_stat = P.n("MM")
        ln_state["stat_mm"][bank] = mm_stat

        def e_rows(v):
            v.tensor_scalar_mul(row(0), ps[st_slot][0:1, :], 1.0 / D)
            v.tensor_scalar_mul(row(512), ps[st2][0:1, :], 1.0 / D)
            v.tensor_tensor(tmp, row(0), row(0), ALU.mult)
            v.tensor_tensor(row(512), row(512), tmp, ALU.subtract)
            return v.tensor_copy(rowb(0), row(0))
        P.op("vector", e_rows,
             waits=[("MM", mm_stat), ("MM", ln_state["bc_mm"][bank])],
             inc=("VEC", 1))
        rows_vec = P.n("VEC")
        slot_free[st_slot] = [("VEC", rows_vec)]
        slot_free[st2] = [("VEC", rows_vec)]

        def e_rstd(s):
            s.activation(tmp, row(512), ACT.Ln, bias=eps_row)
            return s.activation(row(512), tmp, ACT.Exp, scale=neghalf)
        P.op("scalar", e_rstd, waits=[("VEC", rows_vec)], inc=("ACT", 1))
        rstd_act = P.n("ACT")

        def e_bc(t):
            t.matmul(ps[6][:, :], ones_row_f, row(0), start=True, stop=True)
            return t.matmul(ps[7][:, :], ones_row_f, row(512), start=True,
                            stop=True)
        P.op("tensor", e_bc,
             waits=[("ACT", rstd_act)] + slot_free[6] + slot_free[7],
             inc=("MM", 1))
        mm_bc = P.n("MM")
        ln_state["bc_mm"][bank] = mm_bc

        mu_bc = AP(ps[6][:].tensor, ps[6][:].offset,
                   [[ps[6][:].ap[0][0], 128], [0, NJ], [1, BLK]])
        r_bc = AP(ps[7][:].tensor, ps[7][:].offset,
                  [[ps[7][:].ap[0][0], 128], [0, NJ], [1, BLK]])

        def e_apply(v, src3=src3, dst3=dst3, lnt_nm=lnt_nm):
            v.tensor_tensor(AR.ap(lnt_nm, F32, (NJ, BLK)), src3(), mu_bc,
                            ALU.subtract)
            return v.tensor_tensor(dst3(), AR.ap(lnt_nm, F32, (NJ, BLK)),
                                   r_bc, ALU.mult)
        P.op("vector", e_apply, waits=[("MM", mm_bc)], inc=("VEC", 1))
        va = P.n("VEC")
        slot_free[6] = [("VEC", va)]
        slot_free[7] = [("VEC", va)]
        return va

    AR.alloc("wbufA", 16 * 1024)
    AR.alloc("wbufB", 16 * 1024)
    AR.alloc("B", 64 * 1024)
    AR.alloc("xblk", 2 * NJ * BLK * 2)
    AR.alloc("xsq", NJ * BLK * 2)
    AR.alloc("lnt", NJ * BLK * 4)

    def xn1_t(kt, blk):
        return AR.sap("B", BF16, [[1, BLK]], elem_off=kt * TOK + blk * BLK)

    xn1_ready = []
    for blk in range(NBLK):
        buf = blk % 2
        wre = [("VEC", xn1_ready[blk - 2])] if blk >= 2 else []
        xbs = f"XB{buf}"
        P.op("gpsimd", lambda g, blk=blk, buf=buf: g.dma_start(
            out=AR.ap("xblk", BF16, (2, NJ, BLK))[:, buf],
            in_=AP(x_hbm.tensor, blk * BLK,
                   [[TOK, 128], [128 * TOK, NJ], [1, BLK]])),
            waits=wre, inc=(xbs, 16))
        wd = P.n(xbs)
        va = emit_ln3(
            (lambda buf=buf: AR.ap("xblk", BF16, (2, NJ, BLK))[:, buf]),
            (lambda blk=blk: AR.sap("B", BF16, [[TOK, NJ], [1, BLK]],
                                    elem_off=blk * BLK)),
            st_slot=blk % 3, bank=blk % 2,
            src_ready=[(xbs, wd), ("SCL", consts_ld), ("VEC", consts_vec)])
        xn1_ready.append(va)
        P.op("scalar", lambda s, blk=blk: s.dma_start(
            out=AP(xn1sp.tensor, blk * BLK,
                   [[TOK, 128], [128 * TOK, NJ], [1, BLK]]),
            in_=AR.sap("B", BF16, [[TOK, NJ], [1, BLK]], elem_off=blk * BLK)),
            waits=[("VEC", va)], inc=("XN1S", 16))
    xn1_vec = xn1_ready[-1]
    xn1sp_sd = P.n("XN1S")
    p0_done = P.now()
    AR.release("xblk", "xsq", "lnt", "statrows")
    tap("xn1", lambda: AR.ap("B", BF16, (NJ * TOK,)), (128, NJ * TOK), BF16,
        waits=[("VEC", xn1_vec)])

    AR.alloc("C", 64 * 1024)
    AR.alloc("stg", NJ * BLK * 2)
    AR.alloc("vstg", NJ * BLK * 2)
    AR.alloc("fdec", TOK * 2)
    AR.alloc("sdec", TOK * 2)
    AR.alloc("bdec", TOK * 2)

    wnames = ["wbufA", "wbufB"]
    wfree = [[], []]

    wsem = ["WQA", "WQB"]

    def ld_mixw(widx, src, ntile):
        P.op("gpsimd", lambda g, widx=widx, src=src, ntile=ntile: g.dma_start(
            out=AR.ap(wnames[widx], BF16, (ntile, 128)),
            in_=AP(src.tensor, 0, [[128, 128], [128 * 128, ntile], [1, 128]])),
            waits=wfree[widx], inc=(wsem[widx], 16))
        return (wsem[widx], P.n(wsem[widx]))

    def wt(widx, jo, kt):
        return AR.ap(wnames[widx], BF16, (NJ * NJ, 128))[:, jo * NJ + kt]

    def wt1(widx, kt):
        return AR.ap(wnames[widx], BF16, (NJ, 128))[:, kt]

    rq = ld_mixw(0, w_q, NJ * NJ)
    stg_sd = {}
    for jo in range(NJ):
        for blk in range(NBLK):
            slot = (jo * NBLK + blk) % 6

            def e_mm(t, jo=jo, blk=blk, slot=slot):
                last = None
                for kt in range(NJ):
                    last = t.matmul(ps[slot][:, :], wt(0, jo, kt),
                                    xn1_t(kt, blk),
                                    start=(kt == 0), stop=(kt == NJ - 1))
                return last
            P.op("tensor", e_mm,
                 waits=[rq, ("VEC", xn1_ready[blk])] + slot_free[slot],
                 inc=("MM", 1))
            P.op("scalar", lambda s, jo=jo, slot=slot: s.activation(
                AR.ap("stg", BF16, (NJ, BLK))[:, jo], ps[slot][:, :], ACT.Silu),
                waits=[("MM", P.n("MM"))] + stg_sd.get(jo, []),
                inc=("ACT", 1))
            slot_free[slot] = [("ACT", P.n("ACT"))]
            P.op("scalar", lambda s, jo=jo, blk=blk: s.dma_start(
                out=AP(qsp.tensor, jo * 128 * TOK + blk * BLK,
                       [[TOK, 128], [1, BLK]]),
                in_=AR.ap("stg", BF16, (NJ, BLK))[:, jo]),
                inc=(f"QS{jo}", 16))
            stg_sd[jo] = [(f"QS{jo}", P.n(f"QS{jo}"))]
    q_sd = [(f"QS{j}", P.n(f"QS{j}")) for j in range(NJ)]
    wfree[0] = [("MM", P.n("MM"))]

    rk = ld_mixw(1, w_k, NJ * NJ)
    rv = ld_mixw(0, w_v, NJ * NJ)
    kv_guard = {}
    for jo in range(NJ):
        for blk in range(NBLK):
            sk = (jo * NBLK + blk) % 3
            sv = 3 + (jo * NBLK + blk) % 3

            def e_mk(t, jo=jo, blk=blk, slot=sk):
                last = None
                for kt in range(NJ):
                    last = t.matmul(ps[slot][:, :], wt(1, jo, kt),
                                    xn1_t(kt, blk),
                                    start=(kt == 0), stop=(kt == NJ - 1))
                return last
            P.op("tensor", e_mk, waits=[rk, ("VEC", xn1_ready[blk])]
                 + slot_free[sk], inc=("MM", 1))
            mk = P.n("MM")

            def e_mv(t, jo=jo, blk=blk, slot=sv):
                last = None
                for kt in range(NJ):
                    last = t.matmul(ps[slot][:, :], wt(0, jo, kt),
                                    xn1_t(kt, blk),
                                    start=(kt == 0), stop=(kt == NJ - 1))
                return last
            P.op("tensor", e_mv, waits=[rv, ("VEC", xn1_ready[blk])]
                 + slot_free[sv], inc=("MM", 1))
            mv = P.n("MM")
            P.op("scalar", lambda s, jo=jo, slot=sk: s.activation(
                AR.ap("stg", BF16, (NJ, BLK))[:, jo], ps[slot][:, :], ACT.Silu),
                waits=[("MM", mk)] + q_sd + kv_guard.get(jo, []),
                inc=("ACT", 1))
            slot_free[sk] = [("ACT", P.n("ACT"))]
            P.op("scalar", lambda s, jo=jo, slot=sv: s.activation(
                AR.ap("vstg", BF16, (NJ, BLK))[:, jo], ps[slot][:, :], ACT.Silu),
                waits=[("MM", mv)] + kv_guard.get(jo, []),
                inc=("ACT", 1))
            av = P.n("ACT")
            slot_free[sv] = [("ACT", av)]

            def e_kv(v, jo=jo, blk=blk):
                return v.tensor_tensor(
                    AR.sap("C", BF16, [[1, 16], [1024, LC]],
                           elem_off=jo * 128 + blk * 16),
                    AR.sap("stg", BF16, [[LC, 16], [1, LC]],
                           elem_off=jo * BLK),
                    AR.sap("vstg", BF16, [[LC, 16], [1, LC]],
                           elem_off=jo * BLK),
                    ALU.mult)
            P.op("vector", e_kv, waits=[("ACT", av)], inc=("VEC", 1))
            kv_guard[jo] = [("VEC", P.n("VEC"))]
    kv_vec = P.n("VEC")
    wfree[0] = [("MM", P.n("MM"))]
    wfree[1] = [("MM", P.n("MM"))]

    for name, wsrc, bias_off, dstn, widx in (("b", w_b, None, "bdec", 0),
                                             ("f", w_fd, 640, "fdec", 1),
                                             ("s", w_sd, 672, "sdec", 0)):
        rw = ld_mixw(widx, wsrc, NJ)
        for blk in range(NBLK):
            slot = (blk % 3) if widx == 0 else 3 + (blk % 3)

            def e_mm(t, blk=blk, slot=slot, widx=widx):
                last = None
                for kt in range(NJ):
                    last = t.matmul(ps[slot][:, :], wt1(widx, kt),
                                    xn1_t(kt, blk),
                                    start=(kt == 0), stop=(kt == NJ - 1))
                return last
            P.op("tensor", e_mm, waits=[rw, ("VEC", xn1_ready[blk])]
                 + slot_free[slot], inc=("MM", 1))

            def e_ev(s, blk=blk, slot=slot, bias_off=bias_off, dstn=dstn):
                kw = {}
                if bias_off is not None:
                    kw["bias"] = scal(bias_off)
                return s.activation(
                    AR.ap(dstn, BF16, (TOK,))[:, blk * BLK:(blk + 1) * BLK],
                    ps[slot][:, :], ACT.Sigmoid, **kw)
            P.op("scalar", e_ev, waits=[("MM", P.n("MM")), ("SCL", consts_ld)],
                 inc=("ACT", 1))
            slot_free[slot] = [("ACT", P.n("ACT"))]
        wfree[widx] = [("MM", P.n("MM"))]
    dec_act = P.n("ACT")
    mm_p1 = P.n("MM")

    tap("kv", lambda: AR.ap("C", BF16, (NJ * TOK,)), (128, NJ * TOK), BF16,
        waits=[("VEC", kv_vec)])
    tap("fdec", lambda: AR.ap("fdec", BF16, (TOK,)), (128, TOK), BF16,
        waits=[("ACT", dec_act)])
    tap("sdec", lambda: AR.ap("sdec", BF16, (TOK,)), (128, TOK), BF16,
        waits=[("ACT", dec_act)])
    tap("bdec", lambda: AR.ap("bdec", BF16, (TOK,)), (128, TOK), BF16,
        waits=[("ACT", dec_act)])

    p1_done = P.now()
    AR.release("B", "wbufA", "wbufB", "stg", "vstg")
    AR.alloc("wgbuf", 16 * 1024)
    AR.alloc("wobuf", 16 * 1024)
    AR.alloc("scanws", 36 * 1024)
    AR.alloc("prefq", 8 * 1024)
    SW = "scanws"

    def sw(off, n, dtype=F32):
        return AR.ap(SW, dtype, (n,), elem_off=off)

    def swd(off, dims, dtype=F32):
        return AR.sap(SW, dtype, dims, elem_off=off)

    def stb(i, dims=None):
        if dims is None:
            return AR.ap(SW, BF16, (1024,), elem_off=i * 1024)
        return AR.sap(SW, BF16, dims, elem_off=i * 1024)

    JK = [[128, NJ], [1, KC]]

    P.op("gpsimd", lambda g: g.dma_start(
        out=AR.ap("wgbuf", BF16, (NJ * NJ, 128)),
        in_=AP(w_g.tensor, 0, [[128, 128], [128 * 128, NJ * NJ], [1, 128]])),
        waits=p1_done, inc=("WG", 16))
    rg = ("WG", P.n("WG"))
    P.op("gpsimd", lambda g: g.dma_start(
        out=AR.ap("wobuf", BF16, (NJ * NJ, 128)),
        in_=AP(w_o.tensor, 0, [[128, 128], [128 * 128, NJ * NJ], [1, 128]])),
        inc=("WO", 16))
    ro = ("WO", P.n("WO"))

    P.op("vector", lambda v: v.tensor_tensor(
        AR.sap("C", BF16, [[128, NJ], [1, KC], [1024, LC]]),
        AR.sap("C", BF16, [[128, NJ], [1, KC], [1024, LC]]),
        AR.sap("bdec", BF16, [[0, NJ], [LC, KC], [1, LC]]),
        ALU.mult), waits=[("VEC", kv_vec), ("ACT", dec_act)], inc=("VEC", 1))
    u_vec = P.n("VEC")
    tap("u", lambda: AR.ap("C", BF16, (NJ * TOK,)), (128, NJ * TOK), BF16,
        waits=[("VEC", u_vec)])

    def u_l(l):
        return AR.sap("C", BF16, JK, elem_off=l * 1024)

    def f_l(l):
        return AR.sap("fdec", BF16, [[0, NJ], [LC, KC]], elem_off=l)

    def s_l(l):
        return AR.sap("sdec", BF16, [[0, NJ], [LC, KC]], elem_off=l)

    def emit_pass(pass_b, first_waits=()):
        first = True
        for l in range(LC):
            def e_v1(v, l=l, pass_b=pass_b):
                v.tensor_scalar_mul(stb(2), stb(1), MIX)
                v.tensor_scalar_mul(stb(3), stb(0), MIX)
                v.tensor_tensor(stb(2), stb(2), stb(0), ALU.add)
                v.tensor_tensor(stb(3), stb(3), stb(1), ALU.add)
                v.tensor_tensor(stb(4, JK), stb(2, JK), f_l(l), ALU.mult)
                v.tensor_tensor(stb(5, JK), stb(3, JK), s_l(l), ALU.mult)
                v.tensor_tensor(stb(0, JK), stb(4, JK), u_l(l), ALU.add)
                last = v.tensor_tensor(stb(1, JK), stb(5, JK), u_l(l), ALU.add)
                if pass_b:
                    last = v.tensor_tensor(u_l(l), stb(4, JK), stb(5, JK),
                                           ALU.add)
                return last
            w = []
            if first:
                w += list(first_waits) + [("VEC", u_vec)]
                first = False
            P.op("vector", e_v1, waits=w, inc=("VEC", 1))

    P.op("vector", lambda v: (v.memset(stb(0), 0.0),
                              v.memset(stb(1), 0.0))[-1],
         waits=[("VEC", u_vec), ("XN1S", xn1sp_sd)] + q_sd, inc=("VEC", 1))
    emit_pass(False, first_waits=[("ACT", dec_act)])
    passA_gp = P.n("GP")
    passA_vec = P.n("VEC")

    m2 = MIX * MIX

    def tbp(buf, p, dims, extra=0):
        return AR.sap(SW, F32, dims, elem_off=3072 + buf * 2048 + p * 512 + extra)

    for slab in range(4):
        k0 = slab * 32

        def fs_sl(nm, par, k0=k0):
            return AR.sap(nm, BF16, [[LC, 32], [2, 16]], elem_off=k0 * LC + par)

        d1 = [[16, 32], [1, 16]]

        def e_l1(v, k0=k0, fs_sl=fs_sl):
            def cbc(cell):
                return AP(cell.tensor, cell.offset,
                          [[cell.ap[0][0], 128], [0, 32], [0, 16]])
            v.tensor_tensor(tbp(1, 0, d1), fs_sl("fdec", 0), fs_sl("sdec", 0),
                            ALU.add)
            v.tensor_tensor(tbp(1, 1, d1), fs_sl("sdec", 0), cbc(mix2c),
                            ALU.mult)
            v.tensor_tensor(tbp(1, 1, d1), tbp(1, 1, d1), fs_sl("fdec", 0),
                            ALU.add)
            v.tensor_tensor(tbp(1, 2, d1), fs_sl("fdec", 0), cbc(mix2c),
                            ALU.mult)
            v.tensor_tensor(tbp(1, 2, d1), tbp(1, 2, d1), fs_sl("sdec", 0),
                            ALU.add)
            v.tensor_tensor(tbp(0, 0, d1), fs_sl("fdec", 1), tbp(1, 1, d1),
                            ALU.mult)
            v.tensor_tensor(tbp(0, 3, d1), fs_sl("sdec", 1), tbp(1, 2, d1),
                            ALU.mult)
            v.tensor_tensor(tbp(0, 1, d1), fs_sl("fdec", 1), cbc(mixc),
                            ALU.mult)
            v.tensor_tensor(tbp(0, 1, d1), tbp(0, 1, d1), tbp(1, 0, d1),
                            ALU.mult)
            v.tensor_tensor(tbp(0, 2, d1), fs_sl("sdec", 1), cbc(mixc),
                            ALU.mult)
            return v.tensor_tensor(tbp(0, 2, d1), tbp(0, 2, d1),
                                   tbp(1, 0, d1), ALU.mult)
        P.op("gpsimd", e_l1, waits=[("ACT", dec_act), ("VEC", u_vec)],
             inc=("GP", 1))
        n = 16
        buf = 0
        while n > 1:
            nn = n // 2

            def e_lvl(v, buf=buf, n=n, nn=nn):
                def src(p, par):
                    return AR.sap(SW, F32, [[n, 32], [2, nn]],
                                  elem_off=3072 + buf * 2048 + p * 512 + par)

                def dst(p):
                    return AR.sap(SW, F32, [[nn, 32], [1, nn]],
                                  elem_off=3072 + (1 - buf) * 2048 + p * 512)

                def sc(i):
                    return AR.sap(SW, F32, [[nn, 32], [1, nn]],
                                  elem_off=7680 + i * 512)
                last = None
                for ci, b1p, a1p, b2p, a2p in ((0, 0, 0, 1, 2), (1, 0, 1, 1, 3),
                                               (2, 2, 0, 3, 2), (3, 2, 1, 3, 3)):
                    v.tensor_tensor(sc(0), src(b1p, 1), src(a1p, 0), ALU.mult)
                    v.tensor_tensor(sc(1), src(b2p, 1), src(a2p, 0), ALU.mult)
                    last = v.tensor_tensor(dst(ci), sc(0), sc(1), ALU.add)
                return last
            P.op("gpsimd", e_lvl, inc=("GP", 1))
            buf = 1 - buf
            n = nn

        def e_cp(v, buf=buf, k0=k0):
            last = None
            for p in range(4):
                last = v.tensor_copy(
                    sw(7168 + p * KC + k0, 32),
                    AR.sap(SW, F32, [[1, 32]],
                           elem_off=3072 + buf * 2048 + p * 512))
            return last
        P.op("gpsimd", e_cp, inc=("GP", 1))
    atot_gp = P.n("GP")
    atot_vec = P.n("VEC")
    tap("atot", lambda: sw(7168, 512), (128, 512), F32,
        waits=[("VEC", atot_vec)])

    def Pp(buf, p, off=0, n=KC):
        return AR.sap(SW, F32, [[1, n]],
                      elem_off=7680 + buf * 512 + p * KC + off)

    def Qp(buf, comp, off=0, n=KC):
        return AR.sap("prefq", BF16, [[128, NJ], [1, n]],
                      elem_off=buf * 2048 + comp * 1024 + off)

    def e_pi(v):
        for p in range(4):
            v.tensor_copy(Pp(0, p), sw(7168 + p * KC, KC))
        v.tensor_copy(Qp(0, 0), stb(0, JK))
        return v.tensor_copy(Qp(0, 1), stb(1, JK))
    P.op("vector", e_pi, waits=[("GP", atot_gp)], inc=("VEC", 1))

    buf = 0
    d_ = 1
    while d_ < KC:
        def e_hs(v, buf=buf, d_=d_):
            nb = 1 - buf
            n = KC - d_

            def scP(i):
                return AR.sap(SW, F32, [[1, n]], elem_off=5120 + i * 512)

            def scQ(i):
                return AR.sap(SW, F32, [[128, NJ], [1, n]],
                              elem_off=3072 + i * 1024)

            for ci, b1p, a1p, b2p, a2p in ((0, 0, 0, 1, 2), (1, 0, 1, 1, 3),
                                           (2, 2, 0, 3, 2), (3, 2, 1, 3, 3)):
                v.tensor_tensor(scP(0), Pp(buf, b1p, d_, n), Pp(buf, a1p, 0, n),
                                ALU.mult)
                v.tensor_tensor(scP(1), Pp(buf, b2p, d_, n), Pp(buf, a2p, 0, n),
                                ALU.mult)
                v.tensor_tensor(Pp(nb, ci, d_, n), scP(0), scP(1), ALU.add)
            for p in range(4):
                v.tensor_copy(Pp(nb, p, 0, d_), Pp(buf, p, 0, d_))

            def Pb(p):
                return AR.sap(SW, F32, [[0, NJ], [1, n]],
                              elem_off=7680 + buf * 512 + p * KC + d_)
            v.tensor_tensor(scQ(0), Pb(0), Qp(buf, 0, 0, n), ALU.mult)
            v.tensor_tensor(scQ(1), Pb(1), Qp(buf, 1, 0, n), ALU.mult)
            v.tensor_tensor(scQ(0), scQ(0), scQ(1), ALU.add)
            v.tensor_tensor(Qp(nb, 0, d_, n), scQ(0), Qp(buf, 0, d_, n),
                            ALU.add)
            v.tensor_tensor(scQ(0), Pb(2), Qp(buf, 0, 0, n), ALU.mult)
            v.tensor_tensor(scQ(1), Pb(3), Qp(buf, 1, 0, n), ALU.mult)
            v.tensor_tensor(scQ(0), scQ(0), scQ(1), ALU.add)
            v.tensor_tensor(Qp(nb, 1, d_, n), scQ(0), Qp(buf, 1, d_, n),
                            ALU.add)
            v.tensor_copy(Qp(nb, 0, 0, d_), Qp(buf, 0, 0, d_))
            return v.tensor_copy(Qp(nb, 1, 0, d_), Qp(buf, 1, 0, d_))
        P.op("vector", e_hs, inc=("VEC", 1))
        buf = 1 - buf
        d_ *= 2
    pbuf = buf
    pref_vec = P.n("VEC")

    def e_tend(v):
        v.tensor_copy(sw(8704, NJ),
                      AR.sap("prefq", BF16, [[128, NJ], [1, 1]],
                             elem_off=pbuf * 2048 + KC - 1))
        return v.tensor_copy(sw(8712, NJ),
                             AR.sap("prefq", BF16, [[128, NJ], [1, 1]],
                                    elem_off=pbuf * 2048 + 1024 + KC - 1))
    P.op("vector", e_tend, waits=[("VEC", pref_vec)], inc=("VEC", 1))
    P.op("gpsimd", lambda g: g.dma_start(out=tend_b.ap(), in_=sw(8704, 16)),
         waits=[("VEC", P.n("VEC"))], inc=("TB", 16))
    P.op("gpsimd", lambda g: g.collective_compute(
        "AllGather", ALU.bypass,
        replica_groups=[[0, 1], [2, 3], [4, 5], [6, 7]],
        ins=[tend_b.ap().opt()], outs=[tcg_b.ap().opt()]),
        waits=[("TB", P.n("TB"))], inc=("CC", 1))
    P.op("gpsimd", lambda g: g.dma_start(out=sw(8720, 16),
                                         in_=tcg_b.ap()[0:128, :]),
         waits=[("CC", P.n("CC"))], inc=("TB", 16))
    P.op("vector", lambda v: v.tensor_scalar_mul(sw(8720, 16), sw(8720, 16),
                                                 scal(704)),
         waits=[("TB", P.n("TB")), ("SCL", consts_ld)], inc=("VEC", 1))
    tc_vec = P.n("VEC")
    tap("tend", lambda: sw(8704, 16), (128, 16), F32, waits=[("VEC", tc_vec)])
    tap("tc", lambda: sw(8720, 16), (128, 16), F32, waits=[("VEC", tc_vec)])

    def e_t0(v):
        v.tensor_copy(AR.sap(SW, BF16, [[128, NJ], [1, 1]], elem_off=0),
                      AR.sap(SW, F32, [[1, NJ], [1, 1]], elem_off=8720))
        v.tensor_copy(AR.sap(SW, BF16, [[128, NJ], [1, 1]], elem_off=1024),
                      AR.sap(SW, F32, [[1, NJ], [1, 1]], elem_off=8728))
        n = KC - 1

        def Pb(p):
            return AR.sap(SW, F32, [[0, NJ], [1, n]],
                          elem_off=7680 + pbuf * 512 + p * KC)

        def TCc(c):
            return AR.sap(SW, F32, [[1, NJ], [0, n]], elem_off=8720 + c * NJ)

        def scQ(i):
            return AR.sap(SW, F32, [[128, NJ], [1, n]],
                          elem_off=3072 + i * 1024)
        v.tensor_tensor(scQ(0), Pb(0), TCc(0), ALU.mult)
        v.tensor_tensor(scQ(1), Pb(1), TCc(1), ALU.mult)
        v.tensor_tensor(scQ(0), scQ(0), scQ(1), ALU.add)
        v.tensor_tensor(AR.sap(SW, BF16, [[128, NJ], [1, n]], elem_off=1),
                        scQ(0), Qp(pbuf, 0, 0, n), ALU.add)
        v.tensor_tensor(scQ(0), Pb(2), TCc(0), ALU.mult)
        v.tensor_tensor(scQ(1), Pb(3), TCc(1), ALU.mult)
        v.tensor_tensor(scQ(0), scQ(0), scQ(1), ALU.add)
        return v.tensor_tensor(AR.sap(SW, BF16, [[128, NJ], [1, n]],
                                      elem_off=1024 + 1),
                               scQ(0), Qp(pbuf, 1, 0, n), ALU.add)
    P.op("vector", e_t0, waits=[("VEC", tc_vec)], inc=("VEC", 1))
    t0_vec = P.n("VEC")
    tap("t0f", lambda: stb(0), (128, 1024), BF16, waits=[("VEC", t0_vec)])

    emit_pass(True, first_waits=[("VEC", t0_vec)])
    w_vec = P.n("VEC")
    tap("w", lambda: AR.ap("C", BF16, (NJ * TOK,)), (128, NJ * TOK), BF16,
        waits=[("VEC", w_vec)])
    scan_done = [("VEC", w_vec)]

    AR.release("scanws", "prefq", "fdec", "sdec", "bdec")
    AR.alloc("statrows3", 16 * 1024)
    AR.alloc("xn1rl", NJ * BLK * 2)
    AR.alloc("qrl", NJ * BLK * 2)
    AR.alloc("gstg", NJ * BLK * 2)
    AR.alloc("og", NJ * BLK * 2)
    AR.alloc("yb", 2 * NJ * BLK * 2)
    AR.alloc("lnt3", NJ * BLK * 4)
    AR.alloc("x1b", NJ * BLK * 4)
    AR.alloc("xsq3", NJ * BLK * 2)
    RN3 = ("xsq3", "lnt3", "statrows3")

    def wgt(jo, kt):
        return AR.ap("wgbuf", BF16, (NJ * NJ, 128))[:, jo * NJ + kt]

    def wot(jo, kt):
        return AR.ap("wobuf", BF16, (NJ * NJ, 128))[:, jo * NJ + kt]

    prev = {"og": 0, "gact": 0, "yact": 0, "ymm": 0, "wgmm": 0,
            "xn2sd": [0, 0], "x1sd": 0, "xn2va": [0, 0]}
    for blk in range(NBLK):
        pb = blk % 2
        P.op("sync", lambda s, blk=blk: s.dma_start(
            out=AR.ap("xn1rl", BF16, (NJ, BLK)),
            in_=AP(xn1sp.tensor, blk * BLK,
                   [[TOK, 128], [128 * TOK, NJ], [1, BLK]])),
            waits=[("XN1S", xn1sp_sd), ("MM", prev["wgmm"])]
            + (scan_done if blk == 0 else []), inc=("XN1L", 16))
        ldxn = P.n("XN1L")
        P.op("sync", lambda s, blk=blk: s.dma_start(
            out=AR.ap("qrl", BF16, (NJ, BLK)),
            in_=AP(qsp.tensor, blk * BLK,
                   [[TOK, 128], [128 * TOK, NJ], [1, BLK]])),
            waits=q_sd + [("VEC", prev["og"])], inc=("QRL", 16))
        ldq = P.n("QRL")

        for jo in range(NJ):
            slot = (blk * NJ + jo) % 3

            def e_mg(t, jo=jo, slot=slot):
                last = None
                for kt in range(NJ):
                    last = t.matmul(ps[slot][:, :], wgt(jo, kt),
                                    AR.ap("xn1rl", BF16, (NJ, BLK))[:, kt],
                                    start=(kt == 0), stop=(kt == NJ - 1))
                return last
            P.op("tensor", e_mg, waits=[rg, ("XN1L", ldxn)]
                 + slot_free[slot], inc=("MM", 1))
            P.op("scalar", lambda s, jo=jo, slot=slot: s.activation(
                AR.ap("gstg", BF16, (NJ, BLK))[:, jo], ps[slot][:, :],
                ACT.Sigmoid),
                waits=[("MM", P.n("MM")), ("VEC", prev["og"])]
                + (scan_done if blk == 0 else []), inc=("ACT", 1))
            slot_free[slot] = [("ACT", P.n("ACT"))]
        g_act = P.n("ACT")
        prev["wgmm"] = P.n("MM")

        def e_og(v, blk=blk, pb=pb):
            last = None
            for jo in range(NJ):
                v.tensor_tensor(
                    AR.sap("og", BF16, [[LC, 16], [1, LC]],
                           elem_off=jo * BLK),
                    AR.sap("qrl", BF16, [[LC, 16], [1, LC]], elem_off=jo * BLK),
                    AR.sap("C", BF16, [[1, 16], [1024, LC]],
                           elem_off=jo * 128 + blk * 16),
                    ALU.mult)
                last = v.scalar_tensor_tensor(
                    AR.ap("og", BF16, (NJ, BLK))[:, jo],
                    AR.ap("og", BF16, (NJ, BLK))[:, jo], 0.5,
                    AR.ap("gstg", BF16, (NJ, BLK))[:, jo], ALU.mult, ALU.mult)
            return last
        P.op("vector", e_og,
             waits=[("QRL", ldq), ("ACT", g_act), ("XN2S", prev["xn2sd"][0])]
             + scan_done, inc=("VEC", 1))
        og_vec = P.n("VEC")
        prev["og"] = og_vec

        for jo in range(NJ):
            slot = 3 + (blk * NJ + jo) % 3

            def e_my(t, jo=jo, slot=slot, pb=pb):
                last = None
                for kt in range(NJ):
                    last = t.matmul(ps[slot][:, :], wot(jo, kt),
                                    AR.ap("og", BF16, (NJ, BLK))[:, kt],
                                    start=(kt == 0), stop=(kt == NJ - 1))
                return last
            P.op("tensor", e_my, waits=[ro, ("VEC", og_vec)]
                 + slot_free[slot], inc=("MM", 1))
            P.op("scalar", lambda s, jo=jo, slot=slot, pb=pb: s.activation(
                AR.ap("yb", BF16, (2, NJ, BLK))[:, pb, jo], ps[slot][:, :],
                ACT.Identity),
                waits=[("MM", P.n("MM")), ("VEC", prev["xn2va"][pb])]
                + (scan_done if blk == 0 else []), inc=("ACT", 1))
            slot_free[slot] = [("ACT", P.n("ACT"))]
        y_act = P.n("ACT")
        prev["ymm"] = P.n("MM")

        va = emit_ln3(lambda pb=pb: AR.ap("yb", BF16, (2, NJ, BLK))[:, pb],
                      lambda pb=pb: AR.ap("yb", BF16, (2, NJ, BLK))[:, pb],
                      st_slot=blk % 3, bank=blk % 2,
                      src_ready=[("ACT", y_act)], rn=RN3)
        P.op("sync", lambda s, blk=blk: s.dma_start(
            out=AR.ap("lnt3", F32, (NJ, BLK)),
            in_=AP(x_hbm.tensor, blk * BLK,
                   [[TOK, 128], [128 * TOK, NJ], [1, BLK]])),
            waits=[("VEC", va)], inc=("XLD", 16))
        ldx = P.n("XLD")
        P.op("vector", lambda v, pb=pb: v.tensor_tensor(
            AR.ap("x1b", F32, (NJ, BLK)), AR.ap("lnt3", F32, (NJ, BLK)),
            AR.ap("yb", BF16, (2, NJ, BLK))[:, pb], ALU.add),
            waits=[("XLD", ldx), ("X1S", prev["x1sd"])], inc=("VEC", 1))
        x1_vec = P.n("VEC")
        P.op("scalar", lambda s, blk=blk: s.dma_start(
            out=AP(x1sp.tensor, blk * BLK,
                   [[TOK, 128], [128 * TOK, NJ], [1, BLK]]),
            in_=AR.ap("x1b", F32, (NJ, BLK))), waits=[("VEC", x1_vec)],
            inc=("X1S", 16))
        prev["x1sd"] = P.n("X1S")
        P.op("vector", lambda v, pb=pb: v.tensor_copy(
            AR.ap("yb", BF16, (2, NJ, BLK))[:, pb],
            AR.ap("x1b", F32, (NJ, BLK))),
            inc=("VEC", 1))
        cp_vec = P.n("VEC")
        va2 = emit_ln3(lambda pb=pb: AR.ap("yb", BF16, (2, NJ, BLK))[:, pb],
                       lambda: AR.ap("og", BF16, (NJ, BLK)),
                       st_slot=3 + (blk % 3), bank=blk % 2,
                       src_ready=[("VEC", cp_vec), ("MM", prev["ymm"])],
                       rn=RN3)
        prev["xn2va"][pb] = va2
        P.op("scalar", lambda s, blk=blk, pb=pb: s.dma_start(
            out=AP(xn2sp.tensor, blk * BLK,
                   [[TOK, 128], [128 * TOK, NJ], [1, BLK]]),
            in_=AR.ap("og", BF16, (NJ, BLK))), waits=[("VEC", va2)],
            inc=("XN2S", 16))
        prev["xn2sd"][0] = P.n("XN2S")
    xn2_sd = P.n("XN2S")
    x1_sd = P.n("X1S")

    p3_done = P.now()
    AR.release("C", "wgbuf", "wobuf", "xn1rl", "qrl", "gstg", "og", "yb",
               "lnt3", "x1b", "xsq3", "statrows3")
    AR.alloc("B2", 64 * 1024)
    AR.alloc("fw", 4 * NKT * 128 * 2)
    AR.alloc("fstg", 4 * BLK * 2)

    P.op("sync", lambda s: s.dma_start(
        out=AR.ap("B2", BF16, (NJ, TOK)),
        in_=AP(xn2sp.tensor, 0, [[TOK, 128], [128 * TOK, NJ], [1, TOK]])),
        waits=[("XN2S", xn2_sd)] + p3_done, inc=("B2L", 16))
    xn2_ld = P.n("B2L")

    fw_free = [p3_done, p3_done, p3_done, p3_done]

    def ld_fcol(widx, wsrc, fo, nkt, waits=()):
        fsem = f"FW{widx}"

        def e(g, widx=widx, fo=fo, nkt=nkt, wsrc=wsrc):
            return g.dma_start(
                out=AR.ap("fw", BF16, (4, NKT, 128))[:, widx, 0:nkt],
                in_=AP(wsrc.tensor, fo * nkt * 128 * 128,
                       [[128, 128], [128 * 128, nkt], [1, 128]]))
        P.op("gpsimd", e, waits=list(waits) + fw_free[widx], inc=(fsem, 16))
        return (fsem, P.n(fsem))

    def fct(widx, kt):
        return AR.ap("fw", BF16, (4, NKT, 128))[:, widx, kt]

    stg_free = [p3_done, p3_done, p3_done, p3_done]
    for fo in range(NFO):
        wa = (fo % 2) * 2
        wb2 = wa + 1
        r1 = ld_fcol(wa, w_1, fo, NJ)
        r2 = ld_fcol(wb2, w_2, fo, NJ)
        m1l = m2l = 0
        for blk in range(NBLK):
            s1v = blk % 3
            s2v = 3 + blk % 3

            def e_m1(t, blk=blk, wa=wa, slot=s1v):
                last = None
                for kt in range(NJ):
                    last = t.matmul(
                        ps[slot][:, :], fct(wa, kt),
                        AR.ap("B2", BF16, (NJ, TOK))[:, kt,
                                                     blk * BLK:(blk + 1) * BLK],
                        start=(kt == 0), stop=(kt == NJ - 1))
                return last
            P.op("tensor", e_m1, waits=[r1, ("B2L", xn2_ld)]
                 + slot_free[s1v], inc=("MM", 1))
            m1l = P.n("MM")

            def e_m2(t, blk=blk, wb2=wb2, slot=s2v):
                last = None
                for kt in range(NJ):
                    last = t.matmul(
                        ps[slot][:, :], fct(wb2, kt),
                        AR.ap("B2", BF16, (NJ, TOK))[:, kt,
                                                     blk * BLK:(blk + 1) * BLK],
                        start=(kt == 0), stop=(kt == NJ - 1))
                return last
            P.op("tensor", e_m2, waits=[r2] + slot_free[s2v],
                 inc=("MM", 1))
            m2l = P.n("MM")
            stg = blk % 4
            P.op("scalar", lambda s, fo=fo, slot=s1v, stg=stg: s.activation(
                AR.ap("fstg", BF16, (4, BLK))[:, stg], ps[slot][:, :], ACT.Silu,
                scale=scal(0, 32)[:, fo:fo + 1]),
                waits=[("MM", m1l)] + stg_free[stg], inc=("ACT", 1))
            a1v = P.n("ACT")
            slot_free[s1v] = [("ACT", a1v)]
            P.op("vector", lambda v, fo=fo, slot=s2v, stg=stg:
                 v.scalar_tensor_tensor(
                     AR.ap("fstg", BF16, (4, BLK))[:, stg], ps[slot][:, :],
                     scal(128, 32)[:, fo:fo + 1],
                     AR.ap("fstg", BF16, (4, BLK))[:, stg], ALU.mult, ALU.mult),
                 waits=[("MM", m2l), ("ACT", a1v)], inc=("VEC", 1))
            v2v = P.n("VEC")
            slot_free[s2v] = [("VEC", v2v)]
            P.op("scalar", lambda s, fo=fo, blk=blk, stg=stg: s.dma_start(
                out=AP(g1sp.tensor, fo * 128 * TOK + blk * BLK,
                       [[TOK, 128], [1, BLK]]),
                in_=AR.ap("fstg", BF16, (4, BLK))[:, stg]),
                waits=[("VEC", v2v)], inc=(f"FS{stg}", 16))
            stg_free[stg] = [(f"FS{stg}", P.n(f"FS{stg}"))]
        fw_free[wa] = [("MM", m1l)]
        fw_free[wb2] = [("MM", m2l)]
    g1_sd = [(f"FS{i}", P.n(f"FS{i}")) for i in range(4)]

    p4_done = P.now()
    AR.release("B2")
    AR.alloc("rhsA", 64 * SUP)
    AR.alloc("rhsB", 64 * SUP)

    def h_layer2(src_sp, dst_sp, wsrc, scale_off, src_sd_per_sup, extra0):
        mm_after = {}
        caps = []
        mcnt = 0
        for pair in range(NSUP // 2):
            sups = (2 * pair, 2 * pair + 1)
            ld_done = {}
            for idx, sup in enumerate(sups):
                reg = "rhsA" if idx == 0 else "rhsB"
                rsem = "RHA" if idx == 0 else "RHB"
                w0 = list(src_sd_per_sup[sup]) + list(extra0)
                w0.append(("MM", mm_after.get(pair - 1, P.n("MM"))))
                P.op("sync", lambda s, sup=sup, src_sp=src_sp, reg=reg:
                     s.dma_start(
                         out=AR.ap(reg, BF16, (2, 16, SUP))[:, 0],
                         in_=AP(src_sp.tensor, sup * SUP,
                                [[TOK, 128], [128 * TOK, 16], [1, SUP]])),
                     waits=w0, inc=(rsem, 16))
                P.op("scalar", lambda s, sup=sup, src_sp=src_sp, reg=reg:
                     s.dma_start(
                         out=AR.ap(reg, BF16, (2, 16, SUP))[:, 1],
                         in_=AP(src_sp.tensor, 16 * 128 * TOK + sup * SUP,
                                [[TOK, 128], [128 * TOK, 16], [1, SUP]])),
                     waits=w0, inc=(rsem, 16))
                ld_done[sup] = (rsem, P.n(rsem))

            def rhs_t(sup, kt, c0, sups=sups):
                reg = "rhsA" if sup == sups[0] else "rhsB"
                return AR.ap(reg, BF16, (2, 16, SUP))[:, kt // 16, kt % 16,
                                                     c0:c0 + BLK]

            for fo in range(NFO):
                widx = fo % 2
                rw = ld_fcol(widx, wsrc, fo, NKT)
                mml = 0
                for sup in sups:
                    for sb in range(SUP // BLK):
                        slot = mcnt % 6
                        stg = mcnt % 4
                        mcnt += 1

                        def e_mm(t, sup=sup, sb=sb, widx=widx, slot=slot,
                                 rhs_t=rhs_t):
                            last = None
                            for kt in range(NKT):
                                last = t.matmul(ps[slot][:, :], fct(widx, kt),
                                                rhs_t(sup, kt, sb * BLK),
                                                start=(kt == 0),
                                                stop=(kt == NKT - 1))
                            return last
                        P.op("tensor", e_mm, waits=[rw, ld_done[sup]]
                             + slot_free[slot], inc=("MM", 1))
                        mml = P.n("MM")
                        P.op("scalar", lambda s, fo=fo, slot=slot, stg=stg,
                             so=scale_off: s.activation(
                                 AR.ap("fstg", BF16, (4, BLK))[:, stg],
                                 ps[slot][:, :], ACT.Gelu,
                                 scale=scal(so, 32)[:, fo:fo + 1]),
                             waits=[("MM", mml)] + stg_free[stg],
                             inc=("ACT", 1))
                        ev = P.n("ACT")
                        slot_free[slot] = [("ACT", ev)]
                        P.op("scalar", lambda s, fo=fo, sup=sup, sb=sb,
                             stg=stg, dst_sp=dst_sp: s.dma_start(
                                 out=AP(dst_sp.tensor,
                                        fo * 128 * TOK + sup * SUP + sb * BLK,
                                        [[TOK, 128], [1, BLK]]),
                                 in_=AR.ap("fstg", BF16, (4, BLK))[:, stg]),
                             waits=[("ACT", ev)], inc=(f"FS{stg}", 16))
                        stg_free[stg] = [(f"FS{stg}", P.n(f"FS{stg}"))]
                fw_free[widx] = [("MM", mml)]
            mm_after[pair] = P.n("MM")
            pc = [(f"FS{i}", P.n(f"FS{i}")) for i in range(4)]
            caps.append(pc)
            caps.append(pc)
        return caps

    h1_caps = h_layer2(g1sp, h1sp, h_1, 256, [g1_sd] * NSUP, p4_done)
    h2_caps = h_layer2(h1sp, h2sp, h_2, 384, h1_caps, [])
    h2_sd = h2_caps[-1]

    p6_done = P.now()
    AR.release("rhsB")
    AR.alloc("statrows7", 16 * 1024)
    AR.alloc("w3s", 2 * 2 * NJ * BLK * 2)
    AR.alloc("xsq7", NJ * BLK * 2)
    AR.alloc("lnt7", NJ * BLK * 4)
    AR.alloc("x1rl", NJ * BLK * 4)
    RN7 = ("xsq7", "lnt7", "statrows7")

    prev7 = {"out_vec": 0, "out_st": [0, 0], "w3used": [[0, 0], [0, 0]],
             "mmafter": {}}
    for sup in range(NS3):
        sp = sup % 2
        reg7 = "rhsA"
        P.op("sync", lambda s, sup=sup, reg7=reg7: s.dma_start(
            out=AR.ap(reg7, BF16, (NKT, S3)),
            in_=AP(h2sp.tensor, sup * S3,
                   [[TOK, 128], [128 * TOK, NKT], [1, S3]])),
            waits=list(h2_caps[sup])
            + [("MM", prev7["mmafter"].get(sup - 1, P.n("MM")))],
            inc=("RHA", 16))
        ldr = [("RHA", P.n("RHA"))]
        mm_done = []
        for jo in range(NJ):
            widx = 2 + jo % 2
            rw = ld_fcol(widx, w_3, jo, NKT, waits=ldr)
            for sb in range(S3 // BLK):
                slot = (jo * 2 + sb) % 6

                def e_mm(t, jo=jo, sb=sb, widx=widx, slot=slot, reg7=reg7):
                    last = None
                    for kt in range(NKT):
                        last = t.matmul(
                            ps[slot][:, :], fct(widx, kt),
                            AR.ap(reg7, BF16, (NKT, S3))[:, kt,
                                                         sb * BLK:(sb + 1) * BLK],
                            start=(kt == 0), stop=(kt == NKT - 1))
                    return last
                P.op("tensor", e_mm, waits=[rw] + ldr
                     + slot_free[slot], inc=("MM", 1))
                mml = P.n("MM")
                P.op("vector", lambda v, jo=jo, sb=sb, slot=slot, sp=sp:
                     v.tensor_scalar_mul(
                         AR.ap("w3s", BF16, (2, 2, NJ, BLK))[:, sp, sb, jo],
                         ps[slot][:, :], scal(512, 8)[:, jo:jo + 1]),
                     waits=[("MM", mml), ("VEC", prev7["w3used"][sp][sb])]
                     + (p6_done if sup == 0 and jo == 0 else []),
                     inc=("VEC", 1))
                slot_free[slot] = [("VEC", P.n("VEC"))]
                fw_free[widx] = [("MM", mml)]
        w3s_vec = P.n("VEC")
        prev7["mmafter"][sup] = P.n("MM")
        for sb in range(S3 // BLK):
            blk = sup * (S3 // BLK) + sb
            va = emit_ln3(
                lambda sb=sb, sp=sp: AR.ap("w3s", BF16, (2, 2, NJ, BLK))[:, sp, sb],
                lambda sb=sb, sp=sp: AR.ap("w3s", BF16, (2, 2, NJ, BLK))[:, sp, sb],
                st_slot=blk % 3, bank=blk % 2,
                src_ready=[("VEC", w3s_vec)]
                + (p6_done if blk == 0 else []), rn=RN7)
            P.op("sync", lambda s, blk=blk, sb=sb: s.dma_start(
                out=AR.ap("x1rl", F32, (NJ, BLK)),
                in_=AP(x1sp.tensor, blk * BLK,
                       [[TOK, 128], [128 * TOK, NJ], [1, BLK]])),
                waits=[("X1S", x1_sd), ("OUT", max(prev7["out_st"]))]
                + (p6_done if blk == 0 else []), inc=("X1R", 16))
            ldx1 = P.n("X1R")
            P.op("vector", lambda v, sb=sb, sp=sp: v.tensor_tensor(
                AR.ap("x1rl", F32, (NJ, BLK)),
                AR.ap("x1rl", F32, (NJ, BLK)),
                AR.ap("w3s", BF16, (2, 2, NJ, BLK))[:, sp, sb], ALU.add),
                waits=[("X1R", ldx1), ("VEC", va)],
                inc=("VEC", 1))
            ov = P.n("VEC")
            prev7["out_vec"] = ov
            prev7["w3used"][sp][sb] = ov
            P.op("scalar", lambda s, blk=blk, sb=sb: s.dma_start(
                out=AP(out_hbm.tensor, blk * BLK,
                       [[TOK, 128], [128 * TOK, NJ], [1, BLK]]),
                in_=AR.ap("x1rl", F32, (NJ, BLK))),
                waits=[("VEC", ov)], inc=("OUT", 16))
            prev7["out_st"][sb] = P.n("OUT")

    P.op("gpsimd", lambda g: g.nop(),
         waits=[("OUT", P.n("OUT")), ("TAP", P.n("TAP"))], inc=("GP", 1))

    _DRAIN_OPS = frozenset((
        "tensor_tensor", "tensor_add", "tensor_sub", "tensor_mul",
        "tensor_max", "tensor_copy", "tensor_scalar", "tensor_scalar_mul",
        "tensor_scalar_add", "tensor_single_scalar", "scalar_tensor_tensor",
        "tensor_tensor_scan", "reciprocal", "memset", "activation",
        "square", "copy", "mul", "add", "tensor_relu", "select"))

    class _Draining:
        def __init__(self, e):
            self._e = e

        def __getattr__(self, name):
            f = getattr(self._e, name)
            if name in _DRAIN_OPS:
                def g(*a, **k):
                    inst = f(*a, **k)
                    self._e.drain()
                    return inst
                return g
            return f

    with nc.Block() as block:
        def mk(eng_name, wrap):
            def run(e):
                if wrap:
                    e = _Draining(e)
                for emit, waits, inc in P.ops[eng_name]:
                    for s, v in waits:
                        e.wait_ge(sems[s], v)
                    inst = emit(e)
                    if inc is not None and inst is not None:
                        inst.then_inc(sems[inc[0]], inc[1])
            return run
        block.sync(mk("sync", False))
        block.gpsimd(mk("gpsimd", True))
        block.vector(mk("vector", True))
        block.scalar(mk("scalar", True))
        block.tensor(mk("tensor", False))

    nc._keepalive = es
    return nc, tap_d


def _perm():
    idx = np.empty(D, np.int64)
    for j in range(NJ):
        for p in range(128):
            idx[j * 128 + p] = (p // 8) * 64 + 8 * j + (p % 8)
    return idx


def _tile(MT, njo, nkt):
    I, O = MT.shape
    assert I == nkt * 128 and O == njo * 128, (MT.shape, njo, nkt)
    return np.ascontiguousarray(
        MT.reshape(nkt, 128, njo, 128).transpose(2, 0, 1, 3)).astype(np.float32)


def kernel(**inputs):
    x = np.asarray(inputs["L0_attn_x"], np.float32)
    perm = _perm()
    rep = np.arange(128) // 8

    def W(name):
        return np.asarray(inputs[name], np.float32)

    shared = dict(
        wq=_tile(W("L0_attn_Wq").T[:, perm], NJ, NJ),
        wk=_tile(W("L0_attn_Wk").T[:, perm], NJ, NJ),
        wv=_tile(W("L0_attn_Wv").T[:, perm], NJ, NJ),
        wg=_tile(W("L0_attn_Wg").T[:, perm], NJ, NJ),
        wo=_tile(np.ascontiguousarray(W("L0_attn_Wo").T[perm, :]), NJ, NJ),
        wb=_tile(np.ascontiguousarray(W("L0_attn_Wb").T[:, rep]), 1, NJ),
        wfd=_tile(np.ascontiguousarray(W("L0_attn_Wfd").T[:, rep]), 1, NJ),
        wsd=_tile(np.ascontiguousarray(W("L0_attn_Wsd").T[:, rep]), 1, NJ),
        bfd=np.ascontiguousarray(W("L0_attn_bfd")[rep].reshape(128, 1)),
        bsd=np.ascontiguousarray(W("L0_attn_bsd")[rep].reshape(128, 1)),
        w1=_tile(W("L0_ffn_w1_w").T, NFO, NJ),
        w2=_tile(W("L0_ffn_w2_w").T, NFO, NJ),
        h1=_tile(W("L0_ffn_h1_w").T, NFO, NKT),
        h2=_tile(W("L0_ffn_h2_w").T, NFO, NKT),
        w3=_tile(W("L0_ffn_w3_w").T, NJ, NKT),
        s1=np.ascontiguousarray(W("L0_ffn_w1_s").reshape(NFO, 128).T),
        s2=np.ascontiguousarray(W("L0_ffn_w2_s").reshape(NFO, 128).T),
        sh1=np.ascontiguousarray(W("L0_ffn_h1_s").reshape(NFO, 128).T),
        sh2=np.ascontiguousarray(W("L0_ffn_h2_s").reshape(NFO, 128).T),
        s3=np.ascontiguousarray(W("L0_ffn_w3_s").reshape(NJ, 128).T),
    )

    nc, _ = build_nc(DEBUG_TAPS)
    in_maps = []
    for c in range(8):
        b, half = c // 2, c % 2
        in_maps.append({
            **shared,
            "x": np.ascontiguousarray(x[b, half * TOK:(half + 1) * TOK, :].T),
            "mask": np.full((128, 1), float(half), np.float32),
        })

    res = run_bass_kernel_spmd(nc, in_maps, core_ids=list(range(8)))
    out = np.empty((4, 8192, 1024), np.float32)
    for c in range(8):
        b, half = c // 2, c % 2
        out[b, half * TOK:(half + 1) * TOK, :] = res.results[c]["out"].T
    return out


if __name__ == "__main__":
    nc, _ = build_nc()
    ins = {e: len(ops) for e, ops in []} if False else None
    print("built ok")
